# version 05
# baseline (speedup 1.0000x reference)
import numpy as np
import concourse.bass as bass
import concourse.mybir as mybir
from concourse.alu_op_type import AluOpType
from concourse.bass_utils import run_bass_kernel_spmd

F32 = mybir.dt.float32
BF16 = mybir.dt.bfloat16
I32 = mybir.dt.int32
AF = mybir.ActivationFunctionType
AXX = mybir.AxisListType.X
ADD, MUL, SUB = AluOpType.add, AluOpType.mult, AluOpType.subtract
POW, ISGE, ISEQ = AluOpType.pow, AluOpType.is_ge, AluOpType.is_equal

T, D, HID, LAT, NH, HD = 2048, 1024, 4096, 512, 16, 64
S8 = float(8.0 ** -0.5)
EPS = 1e-5
ACH = 512
ENGS = ("T", "V", "S", "G", "Y")


PARANOID = [False]


class P:
    def __init__(self):
        self.ops = {e: [] for e in ENGS}
        self.cnt = {"sT": 0, "sV": 0, "sS": 0, "sG": 0, "dG": 0, "dY": 0}
        self.done = False

    def op(self, eng, fn, waits=(), incs=()):
        if self.done:
            return self.cnt
        if PARANOID[0] and eng in PARANOID[0].split(":")[0]:
            sems = PARANOID[0].split(":")[1] if ":" in PARANOID[0] else "sT,sV,sS,sG,dG,dY"
            waits = list(waits) + [(s, self.cnt[s]) for s in sems.split(",")]
        ws = tuple((s, v) for s, v in waits if v > 0)
        self.ops[eng].append((fn, ws, tuple(incs)))
        for s, n in incs:
            self.cnt[s] += n
        return self.cnt

    def t(self, fn, waits=(), inc=False):
        return self.op("T", fn, waits, (("sT", 1),) if inc else ())["sT"]

    def v(self, fn, waits=(), inc=False, sync=False):
        if sync:
            waits = list(waits) + [("sV", max(1, self.cnt["sV"]))]
        if not inc:
            return self.op("V", fn, waits, ())["sV"]
        self.op("V", fn, waits, ())
        return self.op("V", lambda e: e.drain(), (), (("sV", 1),))["sV"]

    def s(self, fn, waits=(), inc=False):
        if not inc:
            return self.op("S", fn, waits, ())["sS"]
        self.op("S", fn, waits, ())
        return self.op("S", lambda e: e.drain(), (), (("sS", 1),))["sS"]

    def g(self, fn, waits=(), inc=False):
        if not inc:
            return self.op("G", fn, waits, ())["sG"]
        self.op("G", fn, waits, ())
        return self.op("G", lambda e: e.drain(), (), (("sG", 1),))["sG"]

    def gdma(self, fn, waits=()):
        return self.op("G", fn, waits, (("dG", 16),))["dG"]

    def ydma(self, fn, waits=()):
        return self.op("Y", fn, waits, (("dY", 16),))["dY"]

    def m(self, s):
        return self.cnt[s]

    def bar(self, *sems):
        return [(s, self.cnt[s]) for s in sems if self.cnt[s] > 0]


def build_program(dbg=False, upto=99, paranoid=""):
    PARANOID[0] = paranoid
    nc = bass.Bass("TRN2", target_bir_lowering=False,
                   detect_race_conditions=False)

    x_h = nc.dram_tensor("x", [T, D], F32, kind="ExternalInput").ap()
    wc_h = nc.dram_tensor("wc", [D, LAT], F32, kind="ExternalInput").ap()
    wq_h = nc.dram_tensor("wq", [LAT, D], F32, kind="ExternalInput").ap()
    wkv_h = nc.dram_tensor("wkv", [LAT, 2 * HD], F32, kind="ExternalInput").ap()
    wp_h = nc.dram_tensor("wp", [D, D], F32, kind="ExternalInput").ap()
    w1_h = nc.dram_tensor("w1", [HID, D], F32, kind="ExternalInput").ap()
    b1_h = nc.dram_tensor("b1", [HID], F32, kind="ExternalInput").ap()
    w2_h = nc.dram_tensor("w2", [D, HID], F32, kind="ExternalInput").ap()
    b2_h = nc.dram_tensor("b2", [D], F32, kind="ExternalInput").ap()
    scl_h = nc.dram_tensor("scl", [1, 1], F32, kind="ExternalInput").ap()
    y_h = nc.dram_tensor("y", [T, D], F32, kind="ExternalOutput").ap()
    x1sp_h = nc.dram_tensor("x1sp", [T, D], F32).ap()
    if dbg:
        dbg_h = nc.dram_tensor("dbg_h", [128, 16384], BF16,
                               kind="ExternalOutput").ap()
        dbg_o = nc.dram_tensor("dbg_o", [128, 4096], BF16,
                               kind="ExternalOutput").ap()
        dbg_x1c = nc.dram_tensor("dbg_x1c", [128, 4096], BF16,
                                 kind="ExternalOutput").ap()
        dbg_w1t = nc.dram_tensor("dbg_w1t", [128, 32768], BF16,
                                 kind="ExternalOutput").ap()
        dbg_w2t = nc.dram_tensor("dbg_w2t", [128, 32768], BF16,
                                 kind="ExternalOutput").ap()
        dbg_x1 = nc.dram_tensor("dbg_x1", [T, D], F32,
                                kind="ExternalOutput").ap()
        dbg_an = nc.dram_tensor("dbg_an", [T, 4096], BF16,
                                kind="ExternalOutput").ap()
        dbg_yn = nc.dram_tensor("dbg_yn", [128, 4 * 4096], BF16,
                                kind="ExternalOutput").ap()
    x1t_h = nc.dram_tensor("x1t", [128, 8 * T], BF16).ap()

    w1t_o = 0
    xT_o, latT_o, wc_o, wq_o = 0, 16384, 24576, 28672
    w2t_o = 32768
    qT_o, aevT_o, anat_o, x1nb_o = 32768, 49152, 53248, 57344
    C0 = 65536
    wp_o = C0
    kv_o = C0 + 8192
    vex_o = C0 + 10240
    yTn_o = C0 + 11296
    att_o = C0 + 15392
    ynat_o = C0 + 19488
    x1Ts_o = C0 + 20000
    sqs_o = C0 + 24096
    qsq_o = sqs_o
    wst_o = C0 + 25120
    evb_o = C0 + 29216
    hT_o = C0
    x1c_o = C0 + 16384
    oTe_o = C0 + 20480
    sm_o = C0 + 30240
    idn_o = sm_o
    tri_o = sm_o + 128
    sel_o = sm_o + 256
    x1l_o = sm_o + 384
    one_o = sm_o + 512
    NB = sm_o + 520
    ab = nc.alloc_sbuf_tensor("ab", [128, NB], BF16)

    xc_o = 0
    st_o = 4096
    ncm_o = 4120
    x1f_o = 4136
    b1s_o = 4264
    b2s_o = 4296
    scf_o = 4304
    nh_o = 4305
    rs_o = 4309
    rc_o = 4313
    mx_o = 4317
    NF = 4341
    af = nc.alloc_sbuf_tensor("af", [128, NF], F32)

    def bv(off, n, p0=0, p1=128):
        return ab[p0:p1, off:off + n]

    def fv(off, n, p0=0, p1=128):
        return af[p0:p1, off:off + n]

    iotat = nc.alloc_sbuf_tensor("iotat", [128, 128], I32)
    identf = nc.alloc_sbuf_tensor("identf", [32, 32], F32)
    ytme = nc.alloc_sbuf_tensor("ytme", [65, 512], BF16)
    ytmo = nc.alloc_sbuf_tensor("ytmo", [65, 512], BF16)
    sclna = nc.alloc_sbuf_tensor("sclna", [1, 1], F32)
    mrow = nc.alloc_sbuf_tensor("mrow", [1, 16], F32)
    mrowb = nc.alloc_sbuf_tensor("mrowb", [1, 16], BF16)

    pd_e = nc.alloc_psum_tensor("pd_e", [128, 1024], F32)
    pd_o = nc.alloc_psum_tensor("pd_o", [128, 1024], F32)
    py_e = nc.alloc_psum_tensor("py_e", [128, 512], F32)
    py_o = nc.alloc_psum_tensor("py_o", [128, 512], F32)
    pm6 = nc.alloc_psum_tensor("pm6", [128, 512], F32)
    pm7 = nc.alloc_psum_tensor("pm7", [128, 512], F32)

    pr = P()

    def finish():
        pr.op("Y", lambda y: y.nop(),
              [(s, pr.m(s)) for s in ("sT", "sV", "sS", "sG", "dG", "dY")], ())
        pr.done = True


    g_io = pr.g(lambda g: g.iota(iotat[:], [[1, 128]], channel_multiplier=-1,
                                 allow_small_or_imprecise_dtypes=True), inc=True)
    pr.v(lambda v: v.tensor_scalar(bv(idn_o, 128), iotat[:], 0.0, None, ISEQ),
         waits=[("sG", g_io)])
    pr.v(lambda v: v.tensor_scalar(bv(tri_o, 128), iotat[:], 0.0, None, ISGE))
    pr.v(lambda v: v.tensor_scalar(identf[:], iotat[0:32, 0:32], 0.0, None, ISEQ))
    pr.v(lambda v: v.memset(bv(sel_o, 128), 0.0))
    for k in range(8):
        for h in (2 * k, 2 * k + 1):
            pr.v(lambda v, k=k, h=h: v.memset(
                bv(sel_o + 16 * k + h, 1, 64 * (h % 2), 64 * (h % 2) + 64), 1.0))
    pr.v(lambda v: v.memset(bv(one_o, 1), 1.0))
    pr.v(lambda v: v.memset(fv(nh_o, 4), -0.5))
    v_cn = pr.v(lambda v: v.memset(
        ab[:, vex_o:vex_o + 1040].rearrange("p (j c) -> p j c", c=65)[:, :, 64:65],
        1.0), inc=True)

    d_wc = 0
    for k in range(8):
        d_wc = pr.gdma(lambda g, k=k: g.dma_start(
            bv(wc_o + 512 * k, 512), wc_h[128 * k:128 * (k + 1), :]))
    for k in range(4):
        d_wq = pr.gdma(lambda g, k=k: g.dma_start(
            bv(wq_o + 1024 * k, 1024), wq_h[128 * k:128 * (k + 1), :]))
    for k in range(4):
        d_wkv = pr.gdma(lambda g, k=k: g.dma_start(
            bv(qsq_o + 128 * k, 128), wkv_h[128 * k:128 * (k + 1), :]))
    for k in range(8):
        d_wp = pr.gdma(lambda g, k=k: g.dma_start(
            bv(wp_o + 1024 * k, 1024), wp_h[128 * k:128 * (k + 1), :]))
    d_b1 = pr.gdma(lambda g: g.dma_start(
        fv(xc_o, 128, 0, 32), b1_h.rearrange("(m r) -> m r", r=128)))
    d_b2 = pr.gdma(lambda g: g.dma_start(
        fv(xc_o + 128, 128, 0, 8), b2_h.rearrange("(m r) -> m r", r=128)))
    d_sc = pr.gdma(lambda g: g.dma_start(sclna[:], scl_h[:, :]))

    pr.t(lambda t: t.transpose(pm6[0:128, 0:32], fv(xc_o, 128, 0, 32), identf[:]),
         waits=[("dG", d_b2), ("sV", v_cn)])
    t_bt = pr.t(lambda t: t.transpose(pm6[0:128, 32:40], fv(xc_o + 128, 128, 0, 8),
                                      identf[0:8, 0:8]), inc=True)
    pr.s(lambda s: s.copy(fv(b1s_o, 32), pm6[:, 0:32]), waits=[("sT", t_bt)])
    s_b = pr.s(lambda s: s.copy(fv(b2s_o, 8), pm6[:, 32:40]), inc=True)
    v_sc = pr.v(lambda v: v.tensor_copy(mrowb[:, 0:1], sclna[:]),
                waits=[("dG", d_sc)], inc=True)
    t_sc2 = pr.t(lambda t: t.matmul(pm6[:, 40:41], bv(tri_o, 128, 0, 1), mrowb[0:1, 0:1],
                                    start=True, stop=True),
                 waits=[("sV", v_sc), ("sS", s_b)], inc=True)
    s_scl = pr.s(lambda s: s.copy(fv(scf_o, 1), pm6[:, 40:41]),
                 waits=[("sT", t_sc2)], inc=True)

    if upto <= 0.2:
        finish()
    xst_o = x1Ts_o
    s_xT = 0
    t_xgrp = 0
    for gi in range(4):
        d_x = 0
        for tau in range(4):
            w = [("sT", t_xgrp)] if tau == 0 else []
            d_x = pr.gdma(lambda g, gi=gi, tau=tau: g.dma_start(
                bv(xst_o + 1024 * tau, 1024),
                x_h[512 * gi + 128 * tau:512 * gi + 128 * (tau + 1), :]),
                waits=w)
        for k in range(8):
            pm = pm6 if (gi * 8 + k) % 2 == 0 else pm7
            for tau in range(4):
                w = []
                if tau == 0:
                    w = [("dG", d_x)] + pr.bar("sS")
                tm = pr.t(lambda t, pm=pm, tau=tau, k=k: t.matmul(
                    pm[:, 128 * tau:128 * (tau + 1)],
                    bv(xst_o + 1024 * tau + 128 * k, 128), bv(idn_o, 128),
                    start=(tau == 0), stop=(tau == 3)), waits=w,
                    inc=(tau == 3))
            s_xT = pr.s(lambda s, pm=pm, k=k, gi=gi: s.copy(
                bv(xT_o + 2048 * k + 512 * gi, 512), pm[:, 0:512]),
                waits=[("sT", tm)], inc=True)
        t_xgrp = pr.m("sT")

    if upto <= 0.4:
        finish()
    halves = [(pd_e, 0), (pd_e, 512), (pd_o, 0), (pd_o, 512)]
    hstate = {"i": 0, "ev": [0, 0, 0, 0]}

    def next_half():
        i = hstate["i"] % 4
        hstate["i"] += 1
        return i, halves[i][0], halves[i][1]

    def mm_chain(lhs_fn, rhs_fn, nk, evac_fn, extra_waits=()):
        i, pm, c0 = next_half()
        tm = 0
        for k in range(nk):
            w = list(extra_waits) if k == 0 else []
            if k == 0 and hstate["ev"][i]:
                w.append(("sS", hstate["ev"][i]))
            tm = pr.t(lambda t, pm=pm, c0=c0, k=k: t.matmul(
                pm[:, c0:c0 + 512], lhs_fn(k), rhs_fn(k),
                start=(k == 0), stop=(k == nk - 1)), waits=w,
                inc=(k == nk - 1))
        mk = pr.s(lambda s, pm=pm, c0=c0: evac_fn(s, pm, c0),
                  waits=[("sT", tm)], inc=True)
        hstate["ev"][i] = mk
        return mk

    s_lat = 0
    for m in range(4):
        for tc in range(4):
            ew = [("dG", d_wc), ("sS", s_xT)] if (m == 0 and tc == 0) else []
            s_lat = mm_chain(
                lambda k, m=m: bv(wc_o + 512 * k + 128 * m, 128),
                lambda k, tc=tc: bv(xT_o + 2048 * k + 512 * tc, 512), 8,
                lambda s, pm, c0, m=m, tc=tc: s.copy(
                    bv(latT_o + 2048 * m + 512 * tc, 512), pm[:, c0:c0 + 512]),
                ew)
    for m in range(8):
        for tc in range(4):
            ew = [("dG", d_wq), ("sS", s_lat)] if (m == 0 and tc == 0) else []
            mm_chain(
                lambda k, m=m: bv(wq_o + 1024 * k + 128 * m, 128),
                lambda k, tc=tc: bv(latT_o + 2048 * k + 512 * tc, 512), 4,
                lambda s, pm, c0, m=m, tc=tc: s.activation(
                    bv(qT_o + 2048 * m + 512 * tc, 512), pm[:, c0:c0 + 512],
                    AF.Copy, scale=S8), ew)
    if upto <= 0.6:
        finish()
    s_kv = 0
    for tc in range(4):
        iA, pmA, cA = next_half()
        iB, pmB, cB = next_half()
        for k in range(4):
            w = []
            if k == 0:
                w = [("dG", d_wkv), ("sS", s_lat)]
                if hstate["ev"][iA]:
                    w.append(("sS", hstate["ev"][iA]))
            pr.t(lambda t, pmA=pmA, cA=cA, k=k, tc=tc: t.matmul(
                pmA[:, cA:cA + 512], bv(qsq_o + 128 * k, 128),
                bv(latT_o + 2048 * k + 512 * tc, 512),
                start=(k == 0), stop=(k == 3)), waits=w)
        for k in range(4):
            w = []
            if k == 0 and hstate["ev"][iB]:
                w = [("sS", hstate["ev"][iB])]
            tm = pr.t(lambda t, pmB=pmB, cB=cB, k=k, tc=tc: t.matmul(
                pmB[64:128, cB:cB + 512], bv(qsq_o + 128 * k, 64),
                bv(latT_o + 2048 * k + 512 * tc, 512),
                start=(k == 0), stop=(k == 3), tile_position=(0, 64)), waits=w,
                inc=(k == 3))
        pr.s(lambda s, pmA=pmA, cA=cA, tc=tc: s.activation(
            bv(kv_o + 512 * tc, 512, 0, 64), pmA[0:64, cA:cA + 512],
            AF.Copy, scale=S8), waits=[("sT", tm)])
        pr.s(lambda s, pmB=pmB, cB=cB, tc=tc: s.activation(
            bv(kv_o + 512 * tc, 512, 64, 128), pmB[64:128, cB:cB + 512],
            AF.Copy, scale=S8))
        s_kv = pr.s(lambda s, pmA=pmA, cA=cA, tc=tc: s.copy(
            bv(att_o + 512 * tc, 512, 64, 128), pmA[64:128, cA:cA + 512]),
            inc=True)
        hstate["ev"][iA] = s_kv
        hstate["ev"][iB] = s_kv
    if upto <= 0.7:
        finish()
    s_vx = 0
    for jg in range(4):
        pm = pm6 if jg % 2 == 0 else pm7
        for ji in range(4):
            j = 4 * jg + ji
            w = []
            if ji == 0:
                w = [("sS", s_kv)] + pr.bar("sS")
            tm = pr.t(lambda t, pm=pm, ji=ji, j=j: t.matmul(
                pm[:, 64 * ji:64 * (ji + 1)],
                bv(att_o + 128 * j, 128, 64, 128),
                bv(idn_o + 64, 64, 64, 128),
                start=(ji == 0), stop=(ji == 3), tile_position=(64, 0)),
                waits=w, inc=(ji == 3))
        s_vx = pr.s(lambda s, pm=pm, jg=jg: s.copy(
            ab[:, vex_o + 260 * jg:vex_o + 260 * (jg + 1)]
            .rearrange("p (j c) -> p j c", c=65)[:, :, 0:64],
            pm[:, 0:256].rearrange("p (j c) -> p j c", c=64)),
            waits=[("sT", tm)], inc=True)

    if upto <= 0.8:
        finish()
    t_sqmm = {}
    for tc in range(4):
        pm, c0 = [(pd_e, 0), (pd_e, 512), (pd_o, 0), (pd_o, 512)][tc]
        for k in range(8):
            w = [("sS", s_vx)] if (tc == 0 and k == 0) else []
            if (tc, k - 2) in t_sqmm:
                w.append(("sT", t_sqmm[(tc, k - 2)]))
            s_sq = pr.s(lambda s, k=k, tc=tc: s.activation(
                bv(sqs_o + 512 * (k % 2), 512),
                bv(qT_o + 2048 * k + 512 * tc, 512), AF.Square),
                waits=w, inc=True)
            t_sqmm[(tc, k)] = pr.t(lambda t, pm=pm, c0=c0, k=k: t.matmul(
                pm[0:16, c0:c0 + 512], bv(sel_o + 16 * k, 16),
                bv(sqs_o + 512 * (k % 2), 512),
                start=(k == 0), stop=(k == 7)),
                waits=[("sS", s_sq)], inc=True)
        s_sq = pr.s(lambda s, tc=tc: s.activation(
            bv(sqs_o, 512, 0, 64), bv(kv_o + 512 * tc, 512, 0, 64),
            AF.Square), waits=[("sT", t_sqmm[(tc, 7)])], inc=True)
        tm = pr.t(lambda t, tc=tc: t.matmul(
            pm6[0:1, 128 * tc:128 * (tc + 1)], bv(one_o, 1, 0, 64),
            bv(sqs_o, 128, 0, 64), start=(tc == 0), stop=(tc == 3)),
            waits=[("sS", s_sq)], inc=(tc == 3))
    for tc in range(4):
        pm, c0 = [(pd_e, 0), (pd_e, 512), (pd_o, 0), (pd_o, 512)][tc]
        pr.v(lambda v, pm=pm, c0=c0, tc=tc: v.reduce_max(
            fv(mx_o + tc, 1, 0, 16), pm[0:16, c0:c0 + 512], axis=AXX),
            waits=[("sT", tm)] if tc == 0 else [])
    pr.v(lambda v: v.reduce_max(fv(mx_o + 4, 1, 0, 16),
                                fv(mx_o, 4, 0, 16), axis=AXX))
    v_mx = pr.v(lambda v: v.reduce_max(fv(mx_o + 5, 1, 0, 1),
                                       pm6[0:1, 0:512], axis=AXX), inc=True)
    v_of = pr.v(lambda v: v.memset(fv(mx_o + 6, 16, 0, 1), 1.0), inc=True)
    pr.t(lambda t: t.transpose(pm7[0:1, 0:16], fv(mx_o + 4, 1, 0, 16),
                               identf[0:16, 0:16]),
         waits=[("sV", v_mx), ("sV", v_of)] + pr.bar("sS"))
    t_mx = pr.t(lambda t: t.matmul(pm7[0:1, 0:16], fv(mx_o + 5, 1, 0, 1),
                                   fv(mx_o + 6, 16, 0, 1),
                                   start=False, stop=True,
                                   skip_group_check=True), inc=True)
    s_mx = pr.s(lambda s: s.copy(mrow[:, 0:16], pm7[0:1, 0:16]),
                waits=[("sT", t_mx)], inc=True)
    v_nc = pr.v(lambda v: v.tensor_copy(mrowb[:, 0:16], mrow[:, 0:16]),
                waits=[("sS", s_mx)], inc=True)
    t_nc = pr.t(lambda t: t.matmul(pm6[:, 64:80], bv(tri_o, 128, 0, 1),
                                   mrowb[0:1, 0:16],
                                   start=True, stop=True),
                waits=[("sV", v_nc), ("sS", s_mx)], inc=True)
    s_ncm = pr.s(lambda s: s.activation(fv(ncm_o, 16), pm6[:, 64:80],
                                        AF.Copy, scale=-0.5),
                 waits=[("sT", t_nc)], inc=True)

    if upto <= 1:
        if dbg:
            for k in range(8):
                pr.ydma(lambda y, k=k: y.dma_start(
                    dbg_h[:, 2048 * k:2048 * (k + 1)],
                    bv(qT_o + 2048 * k, 2048)),
                    waits=pr.bar("sS", "sV", "sT", "sG"))
            for k in range(4):
                pr.ydma(lambda y, k=k: y.dma_start(
                    dbg_yn[:, 2048 * k:2048 * (k + 1)],
                    bv(latT_o + 2048 * k, 2048)))
            pr.ydma(lambda y: y.dma_start(dbg_yn[:, 8192:10240],
                                          bv(kv_o, 2048)))
            pr.ydma(lambda y: y.dma_start(dbg_yn[:, 10240:11280],
                                          bv(vex_o, 1040)))
            pr.ydma(lambda y: y.dma_start(
                dbg_x1.rearrange("a b -> (a b)")[0:128 * 4341]
                .rearrange("(p q) -> p q", q=4341), fv(0, 4341)))
        finish()

    t_wst_free = [0]

    def emit_w1_group(rg, ev_marks):
        dm = 0
        for ri in range(4):
            r = 4 * rg + ri
            w = [("sT", t_wst_free[0])] if ri == 0 else []
            dm = pr.gdma(lambda g, r=r, ri=ri: g.dma_start(
                bv(wst_o + 1024 * ri, 1024), w1_h[128 * r:128 * (r + 1), :]),
                waits=w)
        for k in range(8):
            pm = pm6 if k % 2 == 0 else pm7
            for ri in range(4):
                w = []
                if ri == 0:
                    w = pr.bar("sV", "sS")
                    if k == 0:
                        w.append(("dG", dm))
                tm2 = pr.t(lambda t, pm=pm, ri=ri, k=k: t.matmul(
                    pm[:, 128 * ri:128 * (ri + 1)],
                    bv(wst_o + 1024 * ri + 128 * k, 128), bv(idn_o, 128),
                    start=(ri == 0), stop=(ri == 3)), waits=w,
                    inc=(ri == 3))
            if k == 7:
                t_wst_free[0] = tm2
            mk = pr.v(lambda v, pm=pm, k=k, rg=rg: v.tensor_copy(
                bv(w1t_o + 4096 * k + 512 * rg, 512), pm[:, 0:512]),
                waits=[("sT", tm2)], inc=True)
            ev_marks.append(mk)

    w1_ev = []

    s_exp = 0
    t_att_hist = []
    v_yTn = 0
    s_proj = 0
    v_ln1 = 0
    s_cv_prev = 0
    dY_x1T_prev = 0
    v_w1eff = 0

    for c in range(4):
        if upto <= 1 + c:
            break
        T0 = ACH * c
        jmax = 4 * c + 3
        d_xc = 0
        for tau in range(4):
            w = []
            if tau == 0 and c > 0:
                w = [("sV", v_ln1), ("sS", s_cv_prev)]
            elif tau == 0:
                w = [("sS", s_scl)]
            d_xc = pr.ydma(lambda y, T0=T0, tau=tau: y.dma_start(
                fv(xc_o + 1024 * tau, 1024),
                x_h[T0 + 128 * tau:T0 + 128 * (tau + 1), :]), waits=w)

        for p in range(8):
            js = list(range(jmax + 1))
            duos = [js[i:i + 2] for i in range(0, len(js), 2)]
            for di, duo in enumerate(duos):
                ab_off = att_o + 1024 * (di % 2)
                for idx, j in enumerate(duo):
                    coff = max(0, 128 * j - T0)
                    cols = ACH - coff
                    w = []
                    if idx == 0:
                        w = [("sS", s_exp)]
                        if s_proj:
                            w.append(("sS", s_proj))
                        if c == 0 and p == 0 and di == 0:
                            w += [("sS", s_ncm), ("sS", s_vx)]
                    pr.t(lambda t, j=j, idx=idx, coff=coff, cols=cols, p=p,
                         T0=T0: t.matmul(
                             pd_e[:, 512 * idx:512 * idx + cols],
                             bv(kv_o + 128 * j, 128, 0, 64),
                             bv(qT_o + 2048 * p + T0 + coff, cols, 0, 64),
                             start=True, stop=True, tile_position=(0, 0)),
                         waits=w)
                    t_sc = pr.t(lambda t, j=j, idx=idx, coff=coff, cols=cols,
                                p=p, T0=T0: t.matmul(
                                    pd_o[:, 512 * idx:512 * idx + cols],
                                    bv(kv_o + 128 * j, 128, 64, 128),
                                    bv(qT_o + 2048 * p + T0 + coff, cols,
                                       64, 128),
                                    start=True, stop=True,
                                    tile_position=(64, 0)),
                                inc=(idx == len(duo) - 1))
                span = 512 * (len(duo) - 1) + ACH - max(0, 128 * duo[-1] - T0)
                we = [("sT", t_sc)]
                if len(t_att_hist) >= 2:
                    we.append(("sT", t_att_hist[-2]))
                pr.s(lambda s, span=span, p=p, ab_off=ab_off: s.activation(
                    bv(ab_off, span), pd_e[:, 0:span], AF.Exp,
                    bias=fv(ncm_o + 2 * p, 1)), waits=we)
                s_exp = pr.s(lambda s, span=span, p=p, ab_off=ab_off:
                             s.activation(bv(ab_off + 2048, span),
                                          pd_o[:, 0:span], AF.Exp,
                                          bias=fv(ncm_o + 2 * p + 1, 1)),
                             inc=True)
                v_mask = 0
                for idx, j in enumerate(duo):
                    if 128 * j >= T0:
                        pr.v(lambda v, idx=idx, ab_off=ab_off: v.tensor_tensor(
                            bv(ab_off + 512 * idx, 128),
                            bv(ab_off + 512 * idx, 128), bv(tri_o, 128), MUL),
                            waits=[("sS", s_exp)])
                        v_mask = pr.v(lambda v, idx=idx, ab_off=ab_off:
                                      v.tensor_tensor(
                                          bv(ab_off + 2048 + 512 * idx, 128),
                                          bv(ab_off + 2048 + 512 * idx, 128),
                                          bv(tri_o, 128), MUL), inc=True)
                for idx, j in enumerate(duo):
                    coff = max(0, 128 * j - T0)
                    cols = ACH - coff
                    w = []
                    if idx == 0:
                        w = [("sS", s_exp)]
                        if v_mask:
                            w.append(("sV", v_mask))
                        if j == 0 and v_yTn:
                            w.append(("sV", v_yTn))
                    pr.t(lambda t, j=j, idx=idx, coff=coff, cols=cols,
                         jmax=jmax, ab_off=ab_off: t.matmul(
                             py_e[0:65, coff:coff + cols],
                             ab[:, vex_o + 65 * j:vex_o + 65 * j + 65],
                             bv(ab_off + 512 * idx, cols),
                             start=(j == 0), stop=(j == jmax)), waits=w)
                    t_av2 = pr.t(lambda t, j=j, idx=idx, coff=coff, cols=cols,
                                 jmax=jmax, ab_off=ab_off: t.matmul(
                                     py_o[0:65, coff:coff + cols],
                                     ab[:, vex_o + 65 * j:vex_o + 65 * j + 65],
                                     bv(ab_off + 2048 + 512 * idx, cols),
                                     start=(j == 0), stop=(j == jmax)),
                                 inc=(idx == len(duo) - 1))
                t_att_hist.append(t_av2)

            t_av = t_att_hist[-1]
            pr.v(lambda v: v.tensor_copy(ytme[:, :], py_e[0:65, :]),
                 waits=[("sT", t_av)])
            v_yt = pr.v(lambda v: v.tensor_copy(ytmo[:, :], py_o[0:65, :]),
                        inc=True)
            for ytmp, pmF in ((ytme, pm6), (ytmo, pm7)):
                for tau in range(4):
                    w = []
                    if tau == 0:
                        w = [("sV", v_yt)] + pr.bar("sS")
                    t_fwd = pr.t(lambda t, ytmp=ytmp, pmF=pmF, tau=tau:
                                 t.matmul(
                                     pmF[:, 65 * tau:65 * (tau + 1)],
                                     ytmp[0:65, 128 * tau:128 * (tau + 1)],
                                     bv(idn_o, 65, 0, 65),
                                     start=(tau == 0), stop=(tau == 3)),
                                 waits=w,
                                 inc=(tau == 3 and pmF is pm7))
            t_bk = 0
            for hh, pmF in ((0, pm6), (1, pm7)):
                yo = ynat_o + 256 * hh
                pr.v(lambda v, pmF=pmF: v.reciprocal(
                    fv(rc_o, 4).rearrange("p (a b) -> p a b", b=1),
                    pmF[:, 0:260].rearrange("p (t c) -> p t c", c=65)
                    [:, :, 64:65]), waits=[("sT", t_fwd)])
                v_nr = 0
                for tau in range(4):
                    v_nr = pr.v(lambda v, pmF=pmF, tau=tau, yo=yo:
                                v.tensor_scalar(
                                    bv(yo + 64 * tau, 64),
                                    pmF[:, 65 * tau:65 * tau + 64],
                                    fv(rc_o + tau, 1), None, MUL), inc=True,
                                sync=(tau == 0))
                for tau in range(4):
                    w = [("sV", v_nr)] if tau == 0 else []
                    t_bk = pr.t(lambda t, tau=tau, hh=hh, yo=yo: t.matmul(
                        py_e[64 * hh:64 * hh + 64, 128 * tau:128 * (tau + 1)],
                        bv(yo + 64 * tau, 64), bv(idn_o, 128),
                        start=(tau == 0), stop=(tau == 3),
                        tile_position=(0, 64 * hh)), waits=w,
                        inc=(tau == 3))
            v_yTn = pr.v(lambda v, p=p: v.tensor_copy(
                bv(yTn_o + 512 * p, 512), py_e[:, 0:512]),
                waits=[("sT", t_bk)], inc=True)

            if c == 0:
                emit_w1_group(p, w1_ev)

        for mo in range(8):
            i, pm, c0 = next_half()
            for k in range(8):
                w = []
                if k == 0:
                    w = [("sV", v_yTn), ("sS", s_exp)]
                    if mo == 0:
                        w.append(("dG", d_wp))
                    if hstate["ev"][i]:
                        w.append(("sS", hstate["ev"][i]))
                tm = pr.t(lambda t, pm=pm, c0=c0, k=k, mo=mo: t.matmul(
                    pm[:, c0:c0 + 512],
                    bv(wp_o + 1024 * k + 128 * mo, 128),
                    bv(yTn_o + 512 * k, 512),
                    start=(k == 0), stop=(k == 7)), waits=w,
                    inc=(k == 7))
            s_proj = pr.s(lambda s, pm=pm, c0=c0, mo=mo: s.copy(
                bv(aevT_o + 512 * mo, 512), pm[:, c0:c0 + 512]),
                waits=[("sT", tm)], inc=True)
            hstate["ev"][i] = s_proj
        v_an = 0
        for tl in range(4):
            for mh in range(2):
                pm = pm6 if (2 * tl + mh) % 2 == 0 else pm7
                for mi in range(4):
                    mo = 4 * mh + mi
                    w = []
                    if mi == 0:
                        w = [("sS", s_proj)] + pr.bar("sV")
                    tm = pr.t(lambda t, pm=pm, mi=mi, mo=mo, tl=tl: t.matmul(
                        pm[:, 128 * mi:128 * (mi + 1)],
                        bv(aevT_o + 512 * mo + 128 * tl, 128), bv(idn_o, 128),
                        start=(mi == 0), stop=(mi == 3)), waits=w,
                        inc=(mi == 3))
                v_an = pr.v(lambda v, pm=pm, tl=tl, mh=mh: v.tensor_copy(
                    bv(anat_o + 1024 * tl + 512 * mh, 512), pm[:, 0:512]),
                    waits=[("sT", tm)], inc=True)
        v_st = 0
        for tau in range(4):
            v_st = pr.v(lambda v, tau=tau: v.scalar_tensor_tensor(
                fv(xc_o + 1024 * tau, 1024), bv(anat_o + 1024 * tau, 1024),
                1.0, fv(xc_o + 1024 * tau, 1024), MUL, ADD,
                accum_out=fv(st_o + tau, 1)),
                waits=[("sV", v_an), ("dY", d_xc)], inc=True)
        s_sq = 0
        for tau in range(4):
            s_sq = pr.s(lambda s, tau=tau: s.activation(
                bv(sqs_o, 1024), fv(xc_o + 1024 * tau, 1024), AF.Square,
                accum_out=fv(st_o + 4 + tau, 1)),
                waits=[("sV", v_st)], inc=True)
        pr.v(lambda v: v.tensor_scalar(fv(st_o + 8, 4), fv(st_o, 4),
                                       1.0 / D, None, MUL),
             waits=[("sS", s_sq)], sync=True)
        pr.v(lambda v: v.tensor_tensor(fv(st_o, 4), fv(st_o + 8, 4),
                                       fv(st_o + 8, 4), MUL), sync=True)
        pr.v(lambda v: v.tensor_scalar(fv(st_o + 4, 4), fv(st_o + 4, 4),
                                       1.0 / D, None, MUL), sync=True)
        v_va = pr.v(lambda v: v.scalar_tensor_tensor(
            fv(st_o + 12, 4), fv(st_o + 4, 4), EPS, fv(st_o, 4), ADD, SUB),
            inc=True, sync=True)
        g_rs = pr.g(lambda g: g.tensor_tensor(fv(rs_o, 4), fv(st_o + 12, 4),
                                              fv(nh_o, 4), POW),
                    waits=[("sV", v_va)], inc=True)
        for tau in range(4):
            w = [("sG", g_rs)] if tau == 0 else []
            v_ln1 = pr.v(lambda v, tau=tau: v.tensor_scalar(
                fv(xc_o + 1024 * tau, 1024), fv(xc_o + 1024 * tau, 1024),
                fv(st_o + 8 + tau, 1), fv(rs_o + tau, 1), SUB, MUL),
                waits=w, inc=True, sync=True)
        for tau in range(4):
            pr.ydma(lambda y, T0=T0, tau=tau: y.dma_start(
                x1sp_h[T0 + 128 * tau:T0 + 128 * (tau + 1), :],
                fv(xc_o + 1024 * tau, 1024)), waits=[("sV", v_ln1)])
        if dbg:
            for tau in range(4):
                pr.ydma(lambda y, T0=T0, tau=tau: y.dma_start(
                    dbg_x1[T0 + 128 * tau:T0 + 128 * (tau + 1), :],
                    fv(xc_o + 1024 * tau, 1024)), waits=[("sV", v_ln1)])
            for tau in range(4):
                pr.ydma(lambda y, T0=T0, tau=tau, c=c: y.dma_start(
                    dbg_an[T0 + 128 * tau:T0 + 128 * (tau + 1), 0:1024]
                    .rearrange("p q -> p q"),
                    bv(anat_o + 1024 * tau, 1024)), waits=[("sV", v_ln1)])
            pr.ydma(lambda y, c=c: y.dma_start(
                dbg_yn[:, 4096 * c:4096 * (c + 1)], bv(yTn_o, 4096)),
                waits=[("sV", v_ln1)])
        s_cv = 0
        for tau in range(4):
            s_cv = pr.s(lambda s, tau=tau: s.copy(
                bv(x1nb_o + 1024 * tau, 1024), fv(xc_o + 1024 * tau, 1024)),
                waits=[("sV", v_ln1)], inc=True)
        s_cv_prev = s_cv
        s_x1T = 0
        for k in range(8):
            pm = pm6 if k % 2 == 0 else pm7
            for tau in range(4):
                w = []
                if tau == 0:
                    w = [("sS", s_cv)] + pr.bar("sS", "sV")
                tm = pr.t(lambda t, pm=pm, tau=tau, k=k: t.matmul(
                    pm[:, 128 * tau:128 * (tau + 1)],
                    bv(x1nb_o + 1024 * tau + 128 * k, 128), bv(idn_o, 128),
                    start=(tau == 0), stop=(tau == 3)), waits=w,
                    inc=(tau == 3))
            w = [("sT", tm)]
            if dY_x1T_prev:
                w.append(("dY", dY_x1T_prev))
            s_x1T = pr.s(lambda s, pm=pm, k=k: s.copy(
                bv(x1Ts_o + 512 * k, 512), pm[:, 0:512]), waits=w, inc=True)
            pr.ydma(lambda y, k=k, T0=T0: y.dma_start(
                x1t_h[:, 2048 * k + T0:2048 * k + T0 + 512],
                bv(x1Ts_o + 512 * k, 512)), waits=[("sS", s_x1T)])
        dY_x1T_prev = pr.m("dY")
        if c == 0:
            v_xl = 0
            for k in range(8):
                v_xl = pr.v(lambda v, k=k: v.tensor_copy(
                    bv(x1l_o + 16 * k, 16), bv(x1Ts_o + 512 * k + 1, 16)),
                    waits=[("sS", s_x1T)], inc=True)
            s_xlf = pr.s(lambda s: s.copy(fv(x1f_o, 128), bv(x1l_o, 128)),
                         waits=[("sV", v_xl)], inc=True)
            for j in range(4):
                for k in range(8):
                    w = []
                    if k == 0:
                        w = [("sS", s_xlf)] + pr.bar("sV", "sS")
                    tm = pr.t(lambda t, k=k, j=j: t.matmul(
                        pm6[0:1, 128 * k:128 * (k + 1)] if k < 4 else
                        pm7[0:1, 128 * (k - 4):128 * (k - 3)],
                        bv(x1l_o + 16 * k + 2 * j, 1), bv(idn_o, 128),
                        start=(k % 4 == 0), stop=(k % 4 == 3)), waits=w,
                        inc=(k == 7))
                pr.s(lambda s: s.copy(bv(wst_o, 512, 0, 1), pm6[0:1, 0:512]),
                     waits=[("sT", tm), ("dY", dY_x1T_prev)])
                s_er = pr.s(lambda s: s.copy(bv(wst_o + 512, 512, 0, 1),
                                             pm7[0:1, 0:512]), inc=True)
                pr.t(lambda t: t.matmul(pm6[:, 0:512], bv(tri_o, 128, 0, 1),
                                        bv(wst_o, 512, 0, 1), start=True,
                                        stop=True), waits=[("sS", s_er)])
                t_eb = pr.t(lambda t: t.matmul(pm7[:, 0:512], bv(tri_o, 128, 0, 1),
                                               bv(wst_o + 512, 512, 0, 1),
                                               start=True, stop=True),
                            inc=True)
                pr.v(lambda v: v.tensor_scalar(bv(evb_o, 512), pm6[:, 0:512],
                                               fv(scf_o, 1), None, MUL),
                     waits=[("sT", t_eb), ("sS", s_scl)])
                v_eb = pr.v(lambda v: v.tensor_scalar(
                    bv(evb_o + 512, 512), pm7[:, 0:512], fv(scf_o, 1), None,
                    MUL), inc=True)
                for k in range(8):
                    w = [("sV", v_eb)]
                    if k == 0 and j == 0 and w1_ev:
                        w.append(("sV", w1_ev[-1]))
                    pr.v(lambda v, k=k, j=j: v.tensor_scalar(
                        bv(sqs_o, 1024), bv(evb_o, 1024),
                        fv(x1f_o + 16 * k + 2 * j + 1, 1), None, MUL),
                        waits=w, sync=True)
                    v_w1eff = pr.v(lambda v, k=k, j=j: v.tensor_tensor(
                        bv(w1t_o + 4096 * k + 1024 * j, 1024),
                        bv(w1t_o + 4096 * k + 1024 * j, 1024),
                        bv(sqs_o, 1024), ADD), inc=True, sync=True)

    if upto <= 5:
        finish()
    t_p2 = pr.m("sT")
    v_p2 = pr.m("sV")
    s_p2 = pr.m("sS")
    dY_p2 = pr.m("dY")

    v_w2t = 0
    for r in range(8):
        w = [("sT", t_wst_free[0])]
        if r == 0:
            w += [("sS", s_p2)]
        dm = pr.gdma(lambda g, r=r: g.dma_start(
            bv(wst_o, 4096), w2_h[128 * r:128 * (r + 1), :]), waits=w)
        for cg in range(8):
            pm = pm6 if cg % 2 == 0 else pm7
            for ci in range(4):
                c2 = 4 * cg + ci
                w = []
                if ci == 0:
                    w = pr.bar("sV", "sS")
                    if cg == 0:
                        w.append(("dG", dm))
                tm2 = pr.t(lambda t, pm=pm, ci=ci, c2=c2: t.matmul(
                    pm[:, 128 * ci:128 * (ci + 1)],
                    bv(wst_o + 128 * c2, 128), bv(idn_o, 128),
                    start=(ci == 0), stop=(ci == 3)), waits=w,
                    inc=(ci == 3))
            if cg == 7:
                t_wst_free[0] = tm2
            base = w2t_o + 4096 * cg + 128 * r
            v_w2t = pr.v(lambda v, pm=pm, base=base: v.tensor_copy(
                ab[:, base:base + 4096].rearrange("p (a b) -> p a b", b=1024)
                [:, :, 0:128],
                pm[:, 0:512].rearrange("p (a b) -> p a b", b=128)),
                waits=[("sT", tm2)], inc=True)

    if upto <= 6:
        finish()

    v_w2eff = 0
    for j in range(4):
        for k in range(8):
            w = []
            if k == 0:
                w = [("sT", t_wst_free[0])] + pr.bar("sV", "sS")
            tm = pr.t(lambda t, k=k, j=j: t.matmul(
                pm6[0:1, 128 * k:128 * (k + 1)] if k < 4 else
                pm7[0:1, 128 * (k - 4):128 * (k - 3)],
                bv(x1l_o + 16 * k + 8 + 2 * j, 1), bv(idn_o, 128),
                start=(k % 4 == 0), stop=(k % 4 == 3)), waits=w,
                inc=(k == 7))
        pr.s(lambda s: s.copy(bv(wst_o, 512, 0, 1), pm6[0:1, 0:512]),
             waits=[("sT", tm)])
        s_er = pr.s(lambda s: s.copy(bv(wst_o + 512, 512, 0, 1), pm7[0:1, 0:512]),
                    inc=True)
        pr.t(lambda t: t.matmul(pm6[:, 0:512], bv(tri_o, 128, 0, 1), bv(wst_o, 512, 0, 1),
                                start=True, stop=True), waits=[("sS", s_er)])
        t_eb = pr.t(lambda t: t.matmul(pm7[:, 0:512], bv(tri_o, 128, 0, 1),
                                       bv(wst_o + 512, 512, 0, 1),
                                       start=True, stop=True), inc=True)
        pr.v(lambda v: v.tensor_scalar(bv(evb_o, 512), pm6[:, 0:512],
                                       fv(scf_o, 1), None, MUL),
             waits=[("sT", t_eb), ("sV", v_w2t)])
        v_eb = pr.v(lambda v: v.tensor_scalar(bv(evb_o + 512, 512),
                                              pm7[:, 0:512], fv(scf_o, 1),
                                              None, MUL), inc=True)
        for c2 in range(32):
            r2, kd = c2 // 8, c2 % 8
            oc = x1f_o + 16 * kd + 9 + 2 * j
            pr.v(lambda v, r2=r2, oc=oc: v.tensor_scalar(
                bv(sqs_o, 256).rearrange("p (a b) -> p a b", b=1),
                ab[:, evb_o:evb_o + 1024].rearrange("p (a b) -> p a b", b=4)
                [:, :, r2:r2 + 1],
                fv(oc, 1), None, MUL),
                waits=[("sV", v_eb)] if c2 == 0 else [], sync=True)
            v_w2eff = pr.v(lambda v, c2=c2, j=j: v.tensor_tensor(
                bv(w2t_o + 1024 * c2 + 256 * j, 256),
                bv(w2t_o + 1024 * c2 + 256 * j, 256), bv(sqs_o, 256), ADD),
                inc=True, sync=True)

    if upto <= 7:
        finish()

    v_ln2 = 0
    d_y_final = 0
    gelu_last = 0
    for fc in range(4):
        if upto <= 7 + fc:
            break
        F0 = ACH * fc
        d_x1c = 0
        for k in range(8):
            w = []
            if fc == 0 and k == 0:
                w = [("sT", t_p2), ("sV", v_p2), ("sS", s_p2)]
            elif k == 0:
                w = [("sS", gelu_last)]
            d_x1c = pr.ydma(lambda y, k=k, F0=F0: y.dma_start(
                bv(x1c_o + 512 * k, 512),
                x1t_h[:, 2048 * k + F0:2048 * k + F0 + 512]), waits=w)
        d_x1r = 0
        for tau in range(4):
            w = []
            if fc == 0 and tau == 0:
                w = [("sS", s_p2)]
            elif tau == 0:
                w = [("sV", v_ln2)]
            d_x1r = pr.ydma(lambda y, tau=tau, F0=F0: y.dma_start(
                fv(xc_o + 1024 * tau, 1024),
                x1sp_h[F0 + 128 * tau:F0 + 128 * (tau + 1), :]), waits=w)
        for mo in range(32):
            i, pm, c0 = next_half()
            for k in range(8):
                w = []
                if k == 0:
                    w = [("dY", d_x1c)]
                    if fc == 0 and mo == 0:
                        w.append(("sV", v_w1eff))
                    if hstate["ev"][i]:
                        w.append(("sS", hstate["ev"][i]))
                tm = pr.t(lambda t, pm=pm, c0=c0, k=k, mo=mo: t.matmul(
                    pm[:, c0:c0 + 512],
                    bv(w1t_o + 4096 * k + 128 * mo, 128),
                    bv(x1c_o + 512 * k, 512),
                    start=(k == 0), stop=(k == 7)), waits=w,
                    inc=(k == 7))
            w = [("sT", tm)]
            if fc == 0 and mo == 0:
                w += [("sS", s_p2), ("sV", v_p2)]
            gelu_last = pr.s(lambda s, pm=pm, c0=c0, mo=mo: s.activation(
                bv(hT_o + 512 * mo, 512), pm[:, c0:c0 + 512], AF.Gelu,
                bias=fv(b1s_o + mo, 1)), waits=w, inc=True)
            hstate["ev"][i] = gelu_last
        if dbg and fc == 0:
            pr.ydma(lambda y: y.dma_start(dbg_h[:, :], bv(hT_o, 16384)),
                    waits=[("sS", gelu_last)])
            pr.ydma(lambda y: y.dma_start(dbg_x1c[:, :], bv(x1c_o, 4096)),
                    waits=[("dY", d_x1c)])
            pr.ydma(lambda y: y.dma_start(dbg_w1t[:, :], bv(w1t_o, 32768)))
            pr.ydma(lambda y: y.dma_start(dbg_w2t[:, :], bv(w2t_o, 32768)))
        s_f2 = 0
        for mo in range(8):
            i, pm, c0 = next_half()
            for c2 in range(32):
                w = []
                if c2 == 0:
                    w = [("sS", gelu_last)]
                    if fc == 0 and mo == 0:
                        w.append(("sV", v_w2eff))
                    if hstate["ev"][i]:
                        w.append(("sS", hstate["ev"][i]))
                tm = pr.t(lambda t, pm=pm, c0=c0, c2=c2, mo=mo: t.matmul(
                    pm[:, c0:c0 + 512],
                    bv(w2t_o + 1024 * c2 + 128 * mo, 128),
                    bv(hT_o + 512 * c2, 512),
                    start=(c2 == 0), stop=(c2 == 31)), waits=w,
                    inc=(c2 == 31))
            w = [("sT", tm)]
            if fc == 0 and mo == 0:
                w.append(("dY", dY_p2))
            s_f2 = pr.s(lambda s, pm=pm, c0=c0, mo=mo: s.activation(
                bv(oTe_o + 512 * mo, 512), pm[:, c0:c0 + 512], AF.Identity,
                bias=fv(b2s_o + mo, 1)), waits=w, inc=True)
            hstate["ev"][i] = s_f2
        if dbg and fc == 0:
            pr.ydma(lambda y: y.dma_start(dbg_o[:, :], bv(oTe_o, 4096)),
                    waits=[("sS", s_f2)])
        v_st = 0
        for tl in range(4):
            for mh in range(2):
                pm = pm6 if (2 * tl + mh) % 2 == 0 else pm7
                for mi in range(4):
                    mo = 4 * mh + mi
                    w = []
                    if mi == 0:
                        w = [("sS", s_f2)] + pr.bar("sV")
                        if fc == 0 and tl == 0 and mh == 0:
                            w.append(("sT", t_wst_free[0]))
                    tm = pr.t(lambda t, pm=pm, mi=mi, mo=mo, tl=tl: t.matmul(
                        pm[:, 128 * mi:128 * (mi + 1)],
                        bv(oTe_o + 512 * mo + 128 * tl, 128), bv(idn_o, 128),
                        start=(mi == 0), stop=(mi == 3)), waits=w,
                        inc=(mi == 3))
                v_st = pr.v(lambda v, pm=pm, tl=tl, mh=mh:
                            v.scalar_tensor_tensor(
                                fv(xc_o + 1024 * tl + 512 * mh, 512),
                                pm[:, 0:512], 1.0,
                                fv(xc_o + 1024 * tl + 512 * mh, 512),
                                MUL, ADD,
                                accum_out=fv(st_o + 16 + 2 * tl + mh, 1)),
                            waits=[("sT", tm), ("dY", d_x1r)], inc=True)
        v_st = pr.v(lambda v: v.tensor_tensor(
            fv(st_o, 4),
            af[:, st_o + 16:st_o + 24].rearrange("p (a b) -> p a b", b=2)
            [:, :, 0:1],
            af[:, st_o + 16:st_o + 24].rearrange("p (a b) -> p a b", b=2)
            [:, :, 1:2], ADD), waits=[("sV", v_st)], inc=True, sync=True)
        s_sq4 = 0
        for tau in range(4):
            w = [("sV", v_st)]
            s_sq4 = pr.s(lambda s, tau=tau: s.activation(
                bv(evb_o, 1024), fv(xc_o + 1024 * tau, 1024), AF.Square,
                accum_out=fv(st_o + 4 + tau, 1)), waits=w, inc=True)
        pr.v(lambda v: v.tensor_scalar(fv(st_o + 8, 4), fv(st_o, 4),
                                       1.0 / D, None, MUL),
             waits=[("sS", s_sq4)], sync=True)
        pr.v(lambda v: v.tensor_tensor(fv(st_o, 4), fv(st_o + 8, 4),
                                       fv(st_o + 8, 4), MUL), sync=True)
        pr.v(lambda v: v.tensor_scalar(fv(st_o + 4, 4), fv(st_o + 4, 4),
                                       1.0 / D, None, MUL), sync=True)
        v_va = pr.v(lambda v: v.scalar_tensor_tensor(
            fv(st_o + 12, 4), fv(st_o + 4, 4), EPS, fv(st_o, 4), ADD, SUB),
            inc=True, sync=True)
        g_rs = pr.g(lambda g: g.tensor_tensor(fv(rs_o, 4), fv(st_o + 12, 4),
                                              fv(nh_o, 4), POW),
                    waits=[("sV", v_va)], inc=True)
        for tau in range(4):
            w = [("sG", g_rs)] if tau == 0 else []
            v_ln2 = pr.v(lambda v, tau=tau: v.tensor_scalar(
                fv(xc_o + 1024 * tau, 1024), fv(xc_o + 1024 * tau, 1024),
                fv(st_o + 8 + tau, 1), fv(rs_o + tau, 1), SUB, MUL),
                waits=w, inc=True, sync=True)
        for tau in range(4):
            d_y_final = pr.ydma(lambda y, F0=F0, tau=tau: y.dma_start(
                y_h[F0 + 128 * tau:F0 + 128 * (tau + 1), :],
                fv(xc_o + 1024 * tau, 1024)), waits=[("sV", v_ln2)])

    if upto > 10:
        pr.op("Y", lambda y: y.nop(), [("dY", d_y_final)], ())
    else:
        finish()

    with (
        nc.semaphore("hsT") as hsT,
        nc.semaphore("hsV") as hsV,
        nc.semaphore("hsS") as hsS,
        nc.semaphore("hsG") as hsG,
        nc.semaphore("hdG") as hdG,
        nc.semaphore("hdY") as hdY,
        nc.Block() as blk,
    ):
        smap = {"sT": hsT, "sV": hsV, "sS": hsS, "sG": hsG,
                "dG": hdG, "dY": hdY}

        def replay(eng_key):
            def inner(eng):
                for fn, waits, incs in pr.ops[eng_key]:
                    for s, vv in waits:
                        eng.wait_ge(smap[s], vv)
                    ins = fn(eng)
                    for s, n in incs:
                        ins.then_inc(smap[s], n)
            return inner

        blk.tensor(replay("T"))
        blk.vector(replay("V"))
        blk.scalar(replay("S"))
        blk.gpsimd(replay("G"))
        blk.sync(replay("Y"))
    return nc


_NC_CACHE = {}


def kernel(**inputs):
    x = np.ascontiguousarray(np.asarray(inputs["L0_attn_x"]))
    wc = np.ascontiguousarray(np.asarray(inputs["L0_attn_Wc_down"]))
    wq = np.ascontiguousarray(np.asarray(inputs["L0_attn_Wq_up"]))
    wkv = np.ascontiguousarray(np.asarray(inputs["L0_attn_Wkv_up"]))
    wp = np.ascontiguousarray(np.asarray(inputs["L0_attn_Wc_proj"]))
    w1 = np.ascontiguousarray(np.asarray(inputs["L0_ffn_W1"]))
    b1 = np.ascontiguousarray(np.asarray(inputs["L0_ffn_b1"]))
    w2 = np.ascontiguousarray(np.asarray(inputs["L0_ffn_W2"]))
    b2 = np.ascontiguousarray(np.asarray(inputs["L0_ffn_b2"]))
    scl = np.ascontiguousarray(np.asarray(inputs["L0_ffn_scale"],
                                          dtype=np.float32).reshape(1, 1))
    if "nc" not in _NC_CACHE:
        _NC_CACHE["nc"] = build_program(paranoid="TVSGY:dG,dY")
    nc = _NC_CACHE["nc"]
    in_maps = [
        {"x": x[i], "wc": wc, "wq": wq, "wkv": wkv, "wp": wp,
         "w1": w1, "b1": b1, "w2": w2, "b2": b2, "scl": scl}
        for i in range(8)
    ]
    res = run_bass_kernel_spmd(nc, in_maps, core_ids=list(range(8)))
    return np.stack([np.asarray(res.results[i]["y"]) for i in range(8)],
                    axis=0)


# version 06
# speedup vs baseline: 1.2353x; 1.2353x over previous
import numpy as np
import concourse.bass as bass
import concourse.mybir as mybir
from concourse.alu_op_type import AluOpType
from concourse.bass_utils import run_bass_kernel_spmd

F32 = mybir.dt.float32
BF16 = mybir.dt.bfloat16
I32 = mybir.dt.int32
AF = mybir.ActivationFunctionType
AXX = mybir.AxisListType.X
ADD, MUL, SUB = AluOpType.add, AluOpType.mult, AluOpType.subtract
POW, ISGE, ISEQ = AluOpType.pow, AluOpType.is_ge, AluOpType.is_equal

T, D, HID, LAT, NH, HD = 2048, 1024, 4096, 512, 16, 64
S8 = float(8.0 ** -0.5)
EPS = 1e-5
ACH = 512
ENGS = ("T", "V", "S", "G", "Y")


PARANOID = [False]


class P:
    def __init__(self):
        self.ops = {e: [] for e in ENGS}
        self.cnt = {"sT": 0, "sV": 0, "sS": 0, "sG": 0, "dG": 0, "dY": 0}
        self.done = False

    def op(self, eng, fn, waits=(), incs=()):
        if self.done:
            return self.cnt
        if PARANOID[0] and eng in PARANOID[0].split(":")[0]:
            sems = PARANOID[0].split(":")[1] if ":" in PARANOID[0] else "sT,sV,sS,sG,dG,dY"
            waits = list(waits) + [(s, self.cnt[s]) for s in sems.split(",")]
        ws = tuple((s, v) for s, v in waits if v > 0)
        self.ops[eng].append((fn, ws, tuple(incs)))
        for s, n in incs:
            self.cnt[s] += n
        return self.cnt

    def t(self, fn, waits=(), inc=False):
        return self.op("T", fn, waits, (("sT", 1),) if inc else ())["sT"]

    def v(self, fn, waits=(), inc=False, sync=False):
        if sync:
            waits = list(waits) + [("sV", max(1, self.cnt["sV"]))]
        if not inc:
            return self.op("V", fn, waits, ())["sV"]
        self.op("V", fn, waits, ())
        return self.op("V", lambda e: e.drain(), (), (("sV", 1),))["sV"]

    def s(self, fn, waits=(), inc=False):
        if not inc:
            return self.op("S", fn, waits, ())["sS"]
        self.op("S", fn, waits, ())
        return self.op("S", lambda e: e.drain(), (), (("sS", 1),))["sS"]

    def g(self, fn, waits=(), inc=False):
        if not inc:
            return self.op("G", fn, waits, ())["sG"]
        self.op("G", fn, waits, ())
        return self.op("G", lambda e: e.drain(), (), (("sG", 1),))["sG"]

    def gdma(self, fn, waits=()):
        return self.op("G", fn, waits, (("dG", 16),))["dG"]

    def ydma(self, fn, waits=()):
        return self.op("Y", fn, waits, (("dY", 16),))["dY"]

    def m(self, s):
        return self.cnt[s]

    def bar(self, *sems):
        return [(s, self.cnt[s]) for s in sems if self.cnt[s] > 0]


def build_program(dbg=False, upto=99, paranoid=""):
    PARANOID[0] = paranoid
    nc = bass.Bass("TRN2", target_bir_lowering=False,
                   detect_race_conditions=False)

    x_h = nc.dram_tensor("x", [T, D], F32, kind="ExternalInput").ap()
    wc_h = nc.dram_tensor("wc", [D, LAT], F32, kind="ExternalInput").ap()
    wq_h = nc.dram_tensor("wq", [LAT, D], F32, kind="ExternalInput").ap()
    wkv_h = nc.dram_tensor("wkv", [LAT, 2 * HD], F32, kind="ExternalInput").ap()
    wp_h = nc.dram_tensor("wp", [D, D], F32, kind="ExternalInput").ap()
    w1_h = nc.dram_tensor("w1", [HID, D], F32, kind="ExternalInput").ap()
    b1_h = nc.dram_tensor("b1", [HID], F32, kind="ExternalInput").ap()
    w2_h = nc.dram_tensor("w2", [D, HID], F32, kind="ExternalInput").ap()
    b2_h = nc.dram_tensor("b2", [D], F32, kind="ExternalInput").ap()
    scl_h = nc.dram_tensor("scl", [1, 1], F32, kind="ExternalInput").ap()
    y_h = nc.dram_tensor("y", [T, D], F32, kind="ExternalOutput").ap()
    x1sp_h = nc.dram_tensor("x1sp", [T, D], F32).ap()
    if dbg:
        dbg_h = nc.dram_tensor("dbg_h", [128, 16384], BF16,
                               kind="ExternalOutput").ap()
        dbg_o = nc.dram_tensor("dbg_o", [128, 4096], BF16,
                               kind="ExternalOutput").ap()
        dbg_x1c = nc.dram_tensor("dbg_x1c", [128, 4096], BF16,
                                 kind="ExternalOutput").ap()
        dbg_w1t = nc.dram_tensor("dbg_w1t", [128, 32768], BF16,
                                 kind="ExternalOutput").ap()
        dbg_w2t = nc.dram_tensor("dbg_w2t", [128, 32768], BF16,
                                 kind="ExternalOutput").ap()
        dbg_x1 = nc.dram_tensor("dbg_x1", [T, D], F32,
                                kind="ExternalOutput").ap()
        dbg_an = nc.dram_tensor("dbg_an", [T, 4096], BF16,
                                kind="ExternalOutput").ap()
        dbg_yn = nc.dram_tensor("dbg_yn", [128, 4 * 4096], BF16,
                                kind="ExternalOutput").ap()
    x1t_h = nc.dram_tensor("x1t", [128, 8 * T], BF16).ap()

    w1t_o = 0
    xT_o, latT_o, wc_o, wq_o = 0, 16384, 24576, 28672
    w2t_o = 32768
    qT_o, aevT_o, anat_o, x1nb_o = 32768, 49152, 53248, 57344
    C0 = 65536
    wp_o = C0
    kv_o = C0 + 8192
    vex_o = C0 + 10240
    yTn_o = C0 + 11296
    att_o = C0 + 15392
    ynat_o = C0 + 19488
    x1Ts_o = C0 + 20000
    sqs_o = C0 + 24096
    qsq_o = sqs_o
    wst_o = C0 + 25120
    evb_o = C0 + 29216
    hT_o = C0
    x1c_o = C0 + 16384
    oTe_o = C0 + 20480
    sm_o = C0 + 30240
    idn_o = sm_o
    tri_o = sm_o + 128
    sel_o = sm_o + 256
    x1l_o = sm_o + 384
    one_o = sm_o + 512
    NB = sm_o + 520
    ab = nc.alloc_sbuf_tensor("ab", [128, NB], BF16)

    xc_o = 0
    st_o = 4096
    ncm_o = 4120
    x1f_o = 4136
    b1s_o = 4264
    b2s_o = 4296
    scf_o = 4304
    nh_o = 4305
    rs_o = 4309
    rc_o = 4313
    mx_o = 4317
    NF = 4341
    af = nc.alloc_sbuf_tensor("af", [128, NF], F32)

    def bv(off, n, p0=0, p1=128):
        return ab[p0:p1, off:off + n]

    def fv(off, n, p0=0, p1=128):
        return af[p0:p1, off:off + n]

    iotat = nc.alloc_sbuf_tensor("iotat", [128, 128], I32)
    identf = nc.alloc_sbuf_tensor("identf", [32, 32], F32)
    ytme = nc.alloc_sbuf_tensor("ytme", [65, 512], BF16)
    ytmo = nc.alloc_sbuf_tensor("ytmo", [65, 512], BF16)
    sclna = nc.alloc_sbuf_tensor("sclna", [1, 1], F32)
    mrow = nc.alloc_sbuf_tensor("mrow", [1, 16], F32)
    mrowb = nc.alloc_sbuf_tensor("mrowb", [1, 16], BF16)

    pd_e = nc.alloc_psum_tensor("pd_e", [128, 1024], F32)
    pd_o = nc.alloc_psum_tensor("pd_o", [128, 1024], F32)
    py_e = nc.alloc_psum_tensor("py_e", [128, 512], F32)
    py_o = nc.alloc_psum_tensor("py_o", [128, 512], F32)
    pm6 = nc.alloc_psum_tensor("pm6", [128, 512], F32)
    pm7 = nc.alloc_psum_tensor("pm7", [128, 512], F32)

    pr = P()

    def finish():
        pr.op("Y", lambda y: y.nop(),
              [(s, pr.m(s)) for s in ("sT", "sV", "sS", "sG", "dG", "dY")], ())
        pr.done = True


    g_io = pr.g(lambda g: g.iota(iotat[:], [[1, 128]], channel_multiplier=-1,
                                 allow_small_or_imprecise_dtypes=True), inc=True)
    pr.v(lambda v: v.tensor_scalar(bv(idn_o, 128), iotat[:], 0.0, None, ISEQ),
         waits=[("sG", g_io)])
    pr.v(lambda v: v.tensor_scalar(bv(tri_o, 128), iotat[:], 0.0, None, ISGE))
    pr.v(lambda v: v.tensor_scalar(identf[:], iotat[0:32, 0:32], 0.0, None, ISEQ))
    pr.v(lambda v: v.memset(bv(sel_o, 128), 0.0))
    for k in range(8):
        for h in (2 * k, 2 * k + 1):
            pr.v(lambda v, k=k, h=h: v.memset(
                bv(sel_o + 16 * k + h, 1, 64 * (h % 2), 64 * (h % 2) + 64), 1.0))
    pr.v(lambda v: v.memset(bv(one_o, 1), 1.0))
    pr.v(lambda v: v.memset(fv(nh_o, 4), -0.5))
    v_cn = pr.v(lambda v: v.memset(
        ab[:, vex_o:vex_o + 1040].rearrange("p (j c) -> p j c", c=65)[:, :, 64:65],
        1.0), inc=True)

    d_wc = 0
    for k in range(8):
        d_wc = pr.gdma(lambda g, k=k: g.dma_start(
            bv(wc_o + 512 * k, 512), wc_h[128 * k:128 * (k + 1), :]))
    for k in range(4):
        d_wq = pr.gdma(lambda g, k=k: g.dma_start(
            bv(wq_o + 1024 * k, 1024), wq_h[128 * k:128 * (k + 1), :]))
    for k in range(4):
        d_wkv = pr.gdma(lambda g, k=k: g.dma_start(
            bv(qsq_o + 128 * k, 128), wkv_h[128 * k:128 * (k + 1), :]))
    for k in range(8):
        d_wp = pr.gdma(lambda g, k=k: g.dma_start(
            bv(wp_o + 1024 * k, 1024), wp_h[128 * k:128 * (k + 1), :]))
    d_b1 = pr.gdma(lambda g: g.dma_start(
        fv(xc_o, 128, 0, 32), b1_h.rearrange("(m r) -> m r", r=128)))
    d_b2 = pr.gdma(lambda g: g.dma_start(
        fv(xc_o + 128, 128, 0, 8), b2_h.rearrange("(m r) -> m r", r=128)))
    d_sc = pr.gdma(lambda g: g.dma_start(sclna[:], scl_h[:, :]))

    pr.t(lambda t: t.transpose(pm6[0:128, 0:32], fv(xc_o, 128, 0, 32), identf[:]),
         waits=[("dG", d_b2), ("sV", v_cn)])
    t_bt = pr.t(lambda t: t.transpose(pm6[0:128, 32:40], fv(xc_o + 128, 128, 0, 8),
                                      identf[0:8, 0:8]), inc=True)
    pr.s(lambda s: s.copy(fv(b1s_o, 32), pm6[:, 0:32]), waits=[("sT", t_bt)])
    s_b = pr.s(lambda s: s.copy(fv(b2s_o, 8), pm6[:, 32:40]), inc=True)
    v_sc = pr.v(lambda v: v.tensor_copy(mrowb[:, 0:1], sclna[:]),
                waits=[("dG", d_sc)], inc=True)
    t_sc2 = pr.t(lambda t: t.matmul(pm6[:, 40:41], bv(tri_o, 128, 0, 1), mrowb[0:1, 0:1],
                                    start=True, stop=True),
                 waits=[("sV", v_sc), ("sS", s_b)], inc=True)
    s_scl = pr.s(lambda s: s.copy(fv(scf_o, 1), pm6[:, 40:41]),
                 waits=[("sT", t_sc2)], inc=True)

    if upto <= 0.2:
        finish()
    xst_o = x1Ts_o
    s_xT = 0
    t_xgrp = 0
    for gi in range(4):
        d_x = 0
        for tau in range(4):
            w = [("sT", t_xgrp)] if tau == 0 else []
            d_x = pr.gdma(lambda g, gi=gi, tau=tau: g.dma_start(
                bv(xst_o + 1024 * tau, 1024),
                x_h[512 * gi + 128 * tau:512 * gi + 128 * (tau + 1), :]),
                waits=w)
        for k in range(8):
            pm = pm6 if (gi * 8 + k) % 2 == 0 else pm7
            for tau in range(4):
                w = []
                if tau == 0:
                    w = [("dG", d_x)] + pr.bar("sS")
                tm = pr.t(lambda t, pm=pm, tau=tau, k=k: t.matmul(
                    pm[:, 128 * tau:128 * (tau + 1)],
                    bv(xst_o + 1024 * tau + 128 * k, 128), bv(idn_o, 128),
                    start=(tau == 0), stop=(tau == 3)), waits=w,
                    inc=(tau == 3))
            s_xT = pr.s(lambda s, pm=pm, k=k, gi=gi: s.copy(
                bv(xT_o + 2048 * k + 512 * gi, 512), pm[:, 0:512]),
                waits=[("sT", tm)], inc=True)
        t_xgrp = pr.m("sT")

    if upto <= 0.4:
        finish()
    halves = [(pd_e, 0), (pd_e, 512), (pd_o, 0), (pd_o, 512)]
    hstate = {"i": 0, "ev": [0, 0, 0, 0]}

    def next_half():
        i = hstate["i"] % 4
        hstate["i"] += 1
        return i, halves[i][0], halves[i][1]

    def mm_chain(lhs_fn, rhs_fn, nk, evac_fn, extra_waits=()):
        i, pm, c0 = next_half()
        tm = 0
        for k in range(nk):
            w = list(extra_waits) if k == 0 else []
            if k == 0 and hstate["ev"][i]:
                w.append(("sS", hstate["ev"][i]))
            tm = pr.t(lambda t, pm=pm, c0=c0, k=k: t.matmul(
                pm[:, c0:c0 + 512], lhs_fn(k), rhs_fn(k),
                start=(k == 0), stop=(k == nk - 1)), waits=w,
                inc=(k == nk - 1))
        mk = pr.s(lambda s, pm=pm, c0=c0: evac_fn(s, pm, c0),
                  waits=[("sT", tm)], inc=True)
        hstate["ev"][i] = mk
        return mk

    s_lat = 0
    for m in range(4):
        for tc in range(4):
            ew = [("dG", d_wc), ("sS", s_xT)] if (m == 0 and tc == 0) else []
            s_lat = mm_chain(
                lambda k, m=m: bv(wc_o + 512 * k + 128 * m, 128),
                lambda k, tc=tc: bv(xT_o + 2048 * k + 512 * tc, 512), 8,
                lambda s, pm, c0, m=m, tc=tc: s.copy(
                    bv(latT_o + 2048 * m + 512 * tc, 512), pm[:, c0:c0 + 512]),
                ew)
    for m in range(8):
        for tc in range(4):
            ew = [("dG", d_wq), ("sS", s_lat)] if (m == 0 and tc == 0) else []
            mm_chain(
                lambda k, m=m: bv(wq_o + 1024 * k + 128 * m, 128),
                lambda k, tc=tc: bv(latT_o + 2048 * k + 512 * tc, 512), 4,
                lambda s, pm, c0, m=m, tc=tc: s.activation(
                    bv(qT_o + 2048 * m + 512 * tc, 512), pm[:, c0:c0 + 512],
                    AF.Copy, scale=S8), ew)
    if upto <= 0.6:
        finish()
    s_kv = 0
    for tc in range(4):
        iA, pmA, cA = next_half()
        iB, pmB, cB = next_half()
        for k in range(4):
            w = []
            if k == 0:
                w = [("dG", d_wkv), ("sS", s_lat)]
                if hstate["ev"][iA]:
                    w.append(("sS", hstate["ev"][iA]))
            pr.t(lambda t, pmA=pmA, cA=cA, k=k, tc=tc: t.matmul(
                pmA[:, cA:cA + 512], bv(qsq_o + 128 * k, 128),
                bv(latT_o + 2048 * k + 512 * tc, 512),
                start=(k == 0), stop=(k == 3)), waits=w)
        for k in range(4):
            w = []
            if k == 0 and hstate["ev"][iB]:
                w = [("sS", hstate["ev"][iB])]
            tm = pr.t(lambda t, pmB=pmB, cB=cB, k=k, tc=tc: t.matmul(
                pmB[64:128, cB:cB + 512], bv(qsq_o + 128 * k, 64),
                bv(latT_o + 2048 * k + 512 * tc, 512),
                start=(k == 0), stop=(k == 3), tile_position=(0, 64)), waits=w,
                inc=(k == 3))
        pr.s(lambda s, pmA=pmA, cA=cA, tc=tc: s.activation(
            bv(kv_o + 512 * tc, 512, 0, 64), pmA[0:64, cA:cA + 512],
            AF.Copy, scale=S8), waits=[("sT", tm)])
        pr.s(lambda s, pmB=pmB, cB=cB, tc=tc: s.activation(
            bv(kv_o + 512 * tc, 512, 64, 128), pmB[64:128, cB:cB + 512],
            AF.Copy, scale=S8))
        s_kv = pr.s(lambda s, pmA=pmA, cA=cA, tc=tc: s.copy(
            bv(att_o + 512 * tc, 512, 64, 128), pmA[64:128, cA:cA + 512]),
            inc=True)
        hstate["ev"][iA] = s_kv
        hstate["ev"][iB] = s_kv
    if upto <= 0.7:
        finish()
    s_vx = 0
    for jg in range(4):
        pm = pm6 if jg % 2 == 0 else pm7
        for ji in range(4):
            j = 4 * jg + ji
            w = []
            if ji == 0:
                w = [("sS", s_kv)] + pr.bar("sS")
            tm = pr.t(lambda t, pm=pm, ji=ji, j=j: t.matmul(
                pm[:, 64 * ji:64 * (ji + 1)],
                bv(att_o + 128 * j, 128, 64, 128),
                bv(idn_o + 64, 64, 64, 128),
                start=(ji == 0), stop=(ji == 3), tile_position=(64, 0)),
                waits=w, inc=(ji == 3))
        s_vx = pr.s(lambda s, pm=pm, jg=jg: s.copy(
            ab[:, vex_o + 260 * jg:vex_o + 260 * (jg + 1)]
            .rearrange("p (j c) -> p j c", c=65)[:, :, 0:64],
            pm[:, 0:256].rearrange("p (j c) -> p j c", c=64)),
            waits=[("sT", tm)], inc=True)

    if upto <= 0.8:
        finish()
    t_sqmm = {}
    for tc in range(4):
        pm, c0 = [(pd_e, 0), (pd_e, 512), (pd_o, 0), (pd_o, 512)][tc]
        for k in range(8):
            w = [("sS", s_vx)] if (tc == 0 and k == 0) else []
            if (tc, k - 2) in t_sqmm:
                w.append(("sT", t_sqmm[(tc, k - 2)]))
            s_sq = pr.s(lambda s, k=k, tc=tc: s.activation(
                bv(sqs_o + 512 * (k % 2), 512),
                bv(qT_o + 2048 * k + 512 * tc, 512), AF.Square),
                waits=w, inc=True)
            t_sqmm[(tc, k)] = pr.t(lambda t, pm=pm, c0=c0, k=k: t.matmul(
                pm[0:16, c0:c0 + 512], bv(sel_o + 16 * k, 16),
                bv(sqs_o + 512 * (k % 2), 512),
                start=(k == 0), stop=(k == 7)),
                waits=[("sS", s_sq)], inc=True)
        s_sq = pr.s(lambda s, tc=tc: s.activation(
            bv(sqs_o, 512, 0, 64), bv(kv_o + 512 * tc, 512, 0, 64),
            AF.Square), waits=[("sT", t_sqmm[(tc, 7)])], inc=True)
        tm = pr.t(lambda t, tc=tc: t.matmul(
            pm6[0:1, 128 * tc:128 * (tc + 1)], bv(one_o, 1, 0, 64),
            bv(sqs_o, 128, 0, 64), start=(tc == 0), stop=(tc == 3)),
            waits=[("sS", s_sq)], inc=(tc == 3))
    for tc in range(4):
        pm, c0 = [(pd_e, 0), (pd_e, 512), (pd_o, 0), (pd_o, 512)][tc]
        pr.v(lambda v, pm=pm, c0=c0, tc=tc: v.reduce_max(
            fv(mx_o + tc, 1, 0, 16), pm[0:16, c0:c0 + 512], axis=AXX),
            waits=[("sT", tm)] if tc == 0 else [])
    pr.v(lambda v: v.reduce_max(fv(mx_o + 4, 1, 0, 16),
                                fv(mx_o, 4, 0, 16), axis=AXX))
    v_mx = pr.v(lambda v: v.reduce_max(fv(mx_o + 5, 1, 0, 1),
                                       pm6[0:1, 0:512], axis=AXX), inc=True)
    v_of = pr.v(lambda v: v.memset(fv(mx_o + 6, 16, 0, 1), 1.0), inc=True)
    pr.t(lambda t: t.transpose(pm7[0:1, 0:16], fv(mx_o + 4, 1, 0, 16),
                               identf[0:16, 0:16]),
         waits=[("sV", v_mx), ("sV", v_of)] + pr.bar("sS"))
    t_mx = pr.t(lambda t: t.matmul(pm7[0:1, 0:16], fv(mx_o + 5, 1, 0, 1),
                                   fv(mx_o + 6, 16, 0, 1),
                                   start=False, stop=True,
                                   skip_group_check=True), inc=True)
    s_mx = pr.s(lambda s: s.copy(mrow[:, 0:16], pm7[0:1, 0:16]),
                waits=[("sT", t_mx)], inc=True)
    v_nc = pr.v(lambda v: v.tensor_copy(mrowb[:, 0:16], mrow[:, 0:16]),
                waits=[("sS", s_mx)], inc=True)
    t_nc = pr.t(lambda t: t.matmul(pm6[:, 64:80], bv(tri_o, 128, 0, 1),
                                   mrowb[0:1, 0:16],
                                   start=True, stop=True),
                waits=[("sV", v_nc), ("sS", s_mx)], inc=True)
    s_ncm = pr.s(lambda s: s.activation(fv(ncm_o, 16), pm6[:, 64:80],
                                        AF.Copy, scale=-0.5),
                 waits=[("sT", t_nc)], inc=True)

    if upto <= 1:
        if dbg:
            for k in range(8):
                pr.ydma(lambda y, k=k: y.dma_start(
                    dbg_h[:, 2048 * k:2048 * (k + 1)],
                    bv(qT_o + 2048 * k, 2048)),
                    waits=pr.bar("sS", "sV", "sT", "sG"))
            for k in range(4):
                pr.ydma(lambda y, k=k: y.dma_start(
                    dbg_yn[:, 2048 * k:2048 * (k + 1)],
                    bv(latT_o + 2048 * k, 2048)))
            pr.ydma(lambda y: y.dma_start(dbg_yn[:, 8192:10240],
                                          bv(kv_o, 2048)))
            pr.ydma(lambda y: y.dma_start(dbg_yn[:, 10240:11280],
                                          bv(vex_o, 1040)))
            pr.ydma(lambda y: y.dma_start(
                dbg_x1.rearrange("a b -> (a b)")[0:128 * 4341]
                .rearrange("(p q) -> p q", q=4341), fv(0, 4341)))
        finish()

    t_wst_free = [0]

    def emit_w1_group(rg, ev_marks):
        dm = 0
        for ri in range(4):
            r = 4 * rg + ri
            w = [("sT", t_wst_free[0])] if ri == 0 else []
            dm = pr.gdma(lambda g, r=r, ri=ri: g.dma_start(
                bv(wst_o + 1024 * ri, 1024), w1_h[128 * r:128 * (r + 1), :]),
                waits=w)
        for k in range(8):
            pm = pm6 if k % 2 == 0 else pm7
            for ri in range(4):
                w = []
                if ri == 0:
                    w = pr.bar("sV", "sS")
                    if k == 0:
                        w.append(("dG", dm))
                tm2 = pr.t(lambda t, pm=pm, ri=ri, k=k: t.matmul(
                    pm[:, 128 * ri:128 * (ri + 1)],
                    bv(wst_o + 1024 * ri + 128 * k, 128), bv(idn_o, 128),
                    start=(ri == 0), stop=(ri == 3)), waits=w,
                    inc=(ri == 3))
            if k == 7:
                t_wst_free[0] = tm2
            mk = pr.v(lambda v, pm=pm, k=k, rg=rg: v.tensor_copy(
                bv(w1t_o + 4096 * k + 512 * rg, 512), pm[:, 0:512]),
                waits=[("sT", tm2)], inc=True)
            ev_marks.append(mk)

    w1_ev = []

    s_exp = 0
    t_att_hist = []
    v_yTn = 0
    s_proj = 0
    v_ln1 = 0
    s_cv_prev = 0
    dY_x1T_prev = 0
    v_w1eff = 0

    for c in range(4):
        if upto <= 1 + c:
            break
        T0 = ACH * c
        jmax = 4 * c + 3
        d_xc = 0
        for tau in range(4):
            w = []
            if tau == 0 and c > 0:
                w = [("sV", v_ln1), ("sS", s_cv_prev)]
            elif tau == 0:
                w = [("sS", s_scl)]
            d_xc = pr.ydma(lambda y, T0=T0, tau=tau: y.dma_start(
                fv(xc_o + 1024 * tau, 1024),
                x_h[T0 + 128 * tau:T0 + 128 * (tau + 1), :]), waits=w)

        for p in range(8):
            js = list(range(jmax + 1))
            duos = [js[i:i + 2] for i in range(0, len(js), 2)]
            for di, duo in enumerate(duos):
                ab_off = att_o + 1024 * (di % 2)
                for idx, j in enumerate(duo):
                    coff = max(0, 128 * j - T0)
                    cols = ACH - coff
                    w = []
                    if idx == 0:
                        w = [("sS", s_exp)]
                        if s_proj:
                            w.append(("sS", s_proj))
                        if c == 0 and p == 0 and di == 0:
                            w += [("sS", s_ncm), ("sS", s_vx)]
                    pr.t(lambda t, j=j, idx=idx, coff=coff, cols=cols, p=p,
                         T0=T0: t.matmul(
                             pd_e[:, 512 * idx:512 * idx + cols],
                             bv(kv_o + 128 * j, 128, 0, 64),
                             bv(qT_o + 2048 * p + T0 + coff, cols, 0, 64),
                             start=True, stop=True, tile_position=(0, 0)),
                         waits=w)
                    t_sc = pr.t(lambda t, j=j, idx=idx, coff=coff, cols=cols,
                                p=p, T0=T0: t.matmul(
                                    pd_o[:, 512 * idx:512 * idx + cols],
                                    bv(kv_o + 128 * j, 128, 64, 128),
                                    bv(qT_o + 2048 * p + T0 + coff, cols,
                                       64, 128),
                                    start=True, stop=True,
                                    tile_position=(64, 0)),
                                inc=(idx == len(duo) - 1))
                span = 512 * (len(duo) - 1) + ACH - max(0, 128 * duo[-1] - T0)
                we = [("sT", t_sc)]
                if len(t_att_hist) >= 2:
                    we.append(("sT", t_att_hist[-2]))
                pr.s(lambda s, span=span, p=p, ab_off=ab_off: s.activation(
                    bv(ab_off, span), pd_e[:, 0:span], AF.Exp,
                    bias=fv(ncm_o + 2 * p, 1)), waits=we)
                s_exp = pr.s(lambda s, span=span, p=p, ab_off=ab_off:
                             s.activation(bv(ab_off + 2048, span),
                                          pd_o[:, 0:span], AF.Exp,
                                          bias=fv(ncm_o + 2 * p + 1, 1)),
                             inc=True)
                v_mask = 0
                for idx, j in enumerate(duo):
                    if 128 * j >= T0:
                        pr.v(lambda v, idx=idx, ab_off=ab_off: v.tensor_tensor(
                            bv(ab_off + 512 * idx, 128),
                            bv(ab_off + 512 * idx, 128), bv(tri_o, 128), MUL),
                            waits=[("sS", s_exp)])
                        v_mask = pr.v(lambda v, idx=idx, ab_off=ab_off:
                                      v.tensor_tensor(
                                          bv(ab_off + 2048 + 512 * idx, 128),
                                          bv(ab_off + 2048 + 512 * idx, 128),
                                          bv(tri_o, 128), MUL), inc=True)
                for idx, j in enumerate(duo):
                    coff = max(0, 128 * j - T0)
                    cols = ACH - coff
                    w = []
                    if idx == 0:
                        w = [("sS", s_exp)]
                        if v_mask:
                            w.append(("sV", v_mask))
                        if j == 0 and v_yTn:
                            w.append(("sV", v_yTn))
                    pr.t(lambda t, j=j, idx=idx, coff=coff, cols=cols,
                         jmax=jmax, ab_off=ab_off: t.matmul(
                             py_e[0:65, coff:coff + cols],
                             ab[:, vex_o + 65 * j:vex_o + 65 * j + 65],
                             bv(ab_off + 512 * idx, cols),
                             start=(j == 0), stop=(j == jmax)), waits=w)
                    t_av2 = pr.t(lambda t, j=j, idx=idx, coff=coff, cols=cols,
                                 jmax=jmax, ab_off=ab_off: t.matmul(
                                     py_o[0:65, coff:coff + cols],
                                     ab[:, vex_o + 65 * j:vex_o + 65 * j + 65],
                                     bv(ab_off + 2048 + 512 * idx, cols),
                                     start=(j == 0), stop=(j == jmax)),
                                 inc=(idx == len(duo) - 1))
                t_att_hist.append(t_av2)

            t_av = t_att_hist[-1]
            pr.v(lambda v: v.tensor_copy(ytme[:, :], py_e[0:65, :]),
                 waits=[("sT", t_av)])
            v_yt = pr.v(lambda v: v.tensor_copy(ytmo[:, :], py_o[0:65, :]),
                        inc=True)
            for ytmp, pmF in ((ytme, pm6), (ytmo, pm7)):
                for tau in range(4):
                    w = []
                    if tau == 0:
                        w = [("sV", v_yt)] + pr.bar("sS")
                    t_fwd = pr.t(lambda t, ytmp=ytmp, pmF=pmF, tau=tau:
                                 t.matmul(
                                     pmF[:, 65 * tau:65 * (tau + 1)],
                                     ytmp[0:65, 128 * tau:128 * (tau + 1)],
                                     bv(idn_o, 65, 0, 65),
                                     start=(tau == 0), stop=(tau == 3)),
                                 waits=w,
                                 inc=(tau == 3 and pmF is pm7))
            t_bk = 0
            for hh, pmF in ((0, pm6), (1, pm7)):
                yo = ynat_o + 256 * hh
                pr.v(lambda v, pmF=pmF: v.reciprocal(
                    fv(rc_o, 4).rearrange("p (a b) -> p a b", b=1),
                    pmF[:, 0:260].rearrange("p (t c) -> p t c", c=65)
                    [:, :, 64:65]), waits=[("sT", t_fwd)])
                v_nr = 0
                for tau in range(4):
                    v_nr = pr.v(lambda v, pmF=pmF, tau=tau, yo=yo:
                                v.tensor_scalar(
                                    bv(yo + 64 * tau, 64),
                                    pmF[:, 65 * tau:65 * tau + 64],
                                    fv(rc_o + tau, 1), None, MUL), inc=True,
                                sync=(tau == 0))
                for tau in range(4):
                    w = [("sV", v_nr)] if tau == 0 else []
                    t_bk = pr.t(lambda t, tau=tau, hh=hh, yo=yo: t.matmul(
                        py_e[64 * hh:64 * hh + 64, 128 * tau:128 * (tau + 1)],
                        bv(yo + 64 * tau, 64), bv(idn_o, 128),
                        start=(tau == 0), stop=(tau == 3),
                        tile_position=(0, 64 * hh)), waits=w,
                        inc=(tau == 3))
            v_yTn = pr.v(lambda v, p=p: v.tensor_copy(
                bv(yTn_o + 512 * p, 512), py_e[:, 0:512]),
                waits=[("sT", t_bk)], inc=True)

            if c == 0:
                emit_w1_group(p, w1_ev)

        for mo in range(8):
            i, pm, c0 = next_half()
            for k in range(8):
                w = []
                if k == 0:
                    w = [("sV", v_yTn), ("sS", s_exp)]
                    if mo == 0:
                        w.append(("dG", d_wp))
                    if hstate["ev"][i]:
                        w.append(("sS", hstate["ev"][i]))
                tm = pr.t(lambda t, pm=pm, c0=c0, k=k, mo=mo: t.matmul(
                    pm[:, c0:c0 + 512],
                    bv(wp_o + 1024 * k + 128 * mo, 128),
                    bv(yTn_o + 512 * k, 512),
                    start=(k == 0), stop=(k == 7)), waits=w,
                    inc=(k == 7))
            s_proj = pr.s(lambda s, pm=pm, c0=c0, mo=mo: s.copy(
                bv(aevT_o + 512 * mo, 512), pm[:, c0:c0 + 512]),
                waits=[("sT", tm)], inc=True)
            hstate["ev"][i] = s_proj
        v_an = 0
        for tl in range(4):
            for mh in range(2):
                pm = pm6 if (2 * tl + mh) % 2 == 0 else pm7
                for mi in range(4):
                    mo = 4 * mh + mi
                    w = []
                    if mi == 0:
                        w = [("sS", s_proj)] + pr.bar("sV")
                    tm = pr.t(lambda t, pm=pm, mi=mi, mo=mo, tl=tl: t.matmul(
                        pm[:, 128 * mi:128 * (mi + 1)],
                        bv(aevT_o + 512 * mo + 128 * tl, 128), bv(idn_o, 128),
                        start=(mi == 0), stop=(mi == 3)), waits=w,
                        inc=(mi == 3))
                v_an = pr.v(lambda v, pm=pm, tl=tl, mh=mh: v.tensor_copy(
                    bv(anat_o + 1024 * tl + 512 * mh, 512), pm[:, 0:512]),
                    waits=[("sT", tm)], inc=True)
        v_st = 0
        for tau in range(4):
            v_st = pr.v(lambda v, tau=tau: v.scalar_tensor_tensor(
                fv(xc_o + 1024 * tau, 1024), bv(anat_o + 1024 * tau, 1024),
                1.0, fv(xc_o + 1024 * tau, 1024), MUL, ADD,
                accum_out=fv(st_o + tau, 1)),
                waits=[("sV", v_an), ("dY", d_xc)], inc=True)
        s_sq = 0
        for tau in range(4):
            s_sq = pr.s(lambda s, tau=tau: s.activation(
                bv(sqs_o, 1024), fv(xc_o + 1024 * tau, 1024), AF.Square,
                accum_out=fv(st_o + 4 + tau, 1)),
                waits=[("sV", v_st)], inc=True)
        pr.v(lambda v: v.tensor_scalar(fv(st_o + 8, 4), fv(st_o, 4),
                                       1.0 / D, None, MUL),
             waits=[("sS", s_sq)], sync=True)
        pr.v(lambda v: v.tensor_tensor(fv(st_o, 4), fv(st_o + 8, 4),
                                       fv(st_o + 8, 4), MUL), sync=True)
        pr.v(lambda v: v.tensor_scalar(fv(st_o + 4, 4), fv(st_o + 4, 4),
                                       1.0 / D, None, MUL), sync=True)
        v_va = pr.v(lambda v: v.scalar_tensor_tensor(
            fv(st_o + 12, 4), fv(st_o + 4, 4), EPS, fv(st_o, 4), ADD, SUB),
            inc=True, sync=True)
        g_rs = pr.g(lambda g: g.tensor_tensor(fv(rs_o, 4), fv(st_o + 12, 4),
                                              fv(nh_o, 4), POW),
                    waits=[("sV", v_va)], inc=True)
        for tau in range(4):
            w = [("sG", g_rs)] if tau == 0 else []
            v_ln1 = pr.v(lambda v, tau=tau: v.tensor_scalar(
                fv(xc_o + 1024 * tau, 1024), fv(xc_o + 1024 * tau, 1024),
                fv(st_o + 8 + tau, 1), fv(rs_o + tau, 1), SUB, MUL),
                waits=w, inc=True, sync=True)
        for tau in range(4):
            pr.ydma(lambda y, T0=T0, tau=tau: y.dma_start(
                x1sp_h[T0 + 128 * tau:T0 + 128 * (tau + 1), :],
                fv(xc_o + 1024 * tau, 1024)), waits=[("sV", v_ln1)])
        if dbg:
            for tau in range(4):
                pr.ydma(lambda y, T0=T0, tau=tau: y.dma_start(
                    dbg_x1[T0 + 128 * tau:T0 + 128 * (tau + 1), :],
                    fv(xc_o + 1024 * tau, 1024)), waits=[("sV", v_ln1)])
            for tau in range(4):
                pr.ydma(lambda y, T0=T0, tau=tau, c=c: y.dma_start(
                    dbg_an[T0 + 128 * tau:T0 + 128 * (tau + 1), 0:1024]
                    .rearrange("p q -> p q"),
                    bv(anat_o + 1024 * tau, 1024)), waits=[("sV", v_ln1)])
            pr.ydma(lambda y, c=c: y.dma_start(
                dbg_yn[:, 4096 * c:4096 * (c + 1)], bv(yTn_o, 4096)),
                waits=[("sV", v_ln1)])
        s_cv = 0
        for tau in range(4):
            s_cv = pr.s(lambda s, tau=tau: s.copy(
                bv(x1nb_o + 1024 * tau, 1024), fv(xc_o + 1024 * tau, 1024)),
                waits=[("sV", v_ln1)], inc=True)
        s_cv_prev = s_cv
        s_x1T = 0
        for k in range(8):
            pm = pm6 if k % 2 == 0 else pm7
            for tau in range(4):
                w = []
                if tau == 0:
                    w = [("sS", s_cv)] + pr.bar("sS", "sV")
                tm = pr.t(lambda t, pm=pm, tau=tau, k=k: t.matmul(
                    pm[:, 128 * tau:128 * (tau + 1)],
                    bv(x1nb_o + 1024 * tau + 128 * k, 128), bv(idn_o, 128),
                    start=(tau == 0), stop=(tau == 3)), waits=w,
                    inc=(tau == 3))
            w = [("sT", tm)]
            if dY_x1T_prev:
                w.append(("dY", dY_x1T_prev))
            s_x1T = pr.s(lambda s, pm=pm, k=k: s.copy(
                bv(x1Ts_o + 512 * k, 512), pm[:, 0:512]), waits=w, inc=True)
            pr.ydma(lambda y, k=k, T0=T0: y.dma_start(
                x1t_h[:, 2048 * k + T0:2048 * k + T0 + 512],
                bv(x1Ts_o + 512 * k, 512)), waits=[("sS", s_x1T)])
        dY_x1T_prev = pr.m("dY")
        if c == 0:
            v_xl = 0
            for k in range(8):
                v_xl = pr.v(lambda v, k=k: v.tensor_copy(
                    bv(x1l_o + 16 * k, 16), bv(x1Ts_o + 512 * k + 1, 16)),
                    waits=[("sS", s_x1T)], inc=True)
            s_xlf = pr.s(lambda s: s.copy(fv(x1f_o, 128), bv(x1l_o, 128)),
                         waits=[("sV", v_xl)], inc=True)
            for j in range(4):
                for k in range(8):
                    w = []
                    if k == 0:
                        w = [("sS", s_xlf)] + pr.bar("sV", "sS")
                    tm = pr.t(lambda t, k=k, j=j: t.matmul(
                        pm6[0:1, 128 * k:128 * (k + 1)] if k < 4 else
                        pm7[0:1, 128 * (k - 4):128 * (k - 3)],
                        bv(x1l_o + 16 * k + 2 * j, 1), bv(idn_o, 128),
                        start=(k % 4 == 0), stop=(k % 4 == 3)), waits=w,
                        inc=(k == 7))
                pr.s(lambda s: s.copy(bv(wst_o, 512, 0, 1), pm6[0:1, 0:512]),
                     waits=[("sT", tm), ("dY", dY_x1T_prev)])
                s_er = pr.s(lambda s: s.copy(bv(wst_o + 512, 512, 0, 1),
                                             pm7[0:1, 0:512]), inc=True)
                pr.t(lambda t: t.matmul(pm6[:, 0:512], bv(tri_o, 128, 0, 1),
                                        bv(wst_o, 512, 0, 1), start=True,
                                        stop=True), waits=[("sS", s_er)])
                t_eb = pr.t(lambda t: t.matmul(pm7[:, 0:512], bv(tri_o, 128, 0, 1),
                                               bv(wst_o + 512, 512, 0, 1),
                                               start=True, stop=True),
                            inc=True)
                pr.v(lambda v: v.tensor_scalar(bv(evb_o, 512), pm6[:, 0:512],
                                               fv(scf_o, 1), None, MUL),
                     waits=[("sT", t_eb), ("sS", s_scl)])
                v_eb = pr.v(lambda v: v.tensor_scalar(
                    bv(evb_o + 512, 512), pm7[:, 0:512], fv(scf_o, 1), None,
                    MUL), inc=True)
                for k in range(8):
                    w = [("sV", v_eb)]
                    if k == 0 and j == 0 and w1_ev:
                        w.append(("sV", w1_ev[-1]))
                    pr.v(lambda v, k=k, j=j: v.tensor_scalar(
                        bv(sqs_o, 1024), bv(evb_o, 1024),
                        fv(x1f_o + 16 * k + 2 * j + 1, 1), None, MUL),
                        waits=w, sync=True)
                    v_w1eff = pr.v(lambda v, k=k, j=j: v.tensor_tensor(
                        bv(w1t_o + 4096 * k + 1024 * j, 1024),
                        bv(w1t_o + 4096 * k + 1024 * j, 1024),
                        bv(sqs_o, 1024), ADD), inc=True, sync=True)

    if upto <= 5:
        finish()
    t_p2 = pr.m("sT")
    v_p2 = pr.m("sV")
    s_p2 = pr.m("sS")
    dY_p2 = pr.m("dY")

    v_w2t = 0
    for r in range(8):
        w = [("sT", t_wst_free[0])]
        if r == 0:
            w += [("sS", s_p2)]
        dm = pr.gdma(lambda g, r=r: g.dma_start(
            bv(wst_o, 4096), w2_h[128 * r:128 * (r + 1), :]), waits=w)
        for cg in range(8):
            pm = pm6 if cg % 2 == 0 else pm7
            for ci in range(4):
                c2 = 4 * cg + ci
                w = []
                if ci == 0:
                    w = pr.bar("sV", "sS")
                    if cg == 0:
                        w.append(("dG", dm))
                tm2 = pr.t(lambda t, pm=pm, ci=ci, c2=c2: t.matmul(
                    pm[:, 128 * ci:128 * (ci + 1)],
                    bv(wst_o + 128 * c2, 128), bv(idn_o, 128),
                    start=(ci == 0), stop=(ci == 3)), waits=w,
                    inc=(ci == 3))
            if cg == 7:
                t_wst_free[0] = tm2
            base = w2t_o + 4096 * cg + 128 * r
            v_w2t = pr.v(lambda v, pm=pm, base=base: v.tensor_copy(
                ab[:, base:base + 4096].rearrange("p (a b) -> p a b", b=1024)
                [:, :, 0:128],
                pm[:, 0:512].rearrange("p (a b) -> p a b", b=128)),
                waits=[("sT", tm2)], inc=True)

    if upto <= 6:
        finish()

    v_w2eff = 0
    for j in range(4):
        for k in range(8):
            w = []
            if k == 0:
                w = [("sT", t_wst_free[0])] + pr.bar("sV", "sS")
            tm = pr.t(lambda t, k=k, j=j: t.matmul(
                pm6[0:1, 128 * k:128 * (k + 1)] if k < 4 else
                pm7[0:1, 128 * (k - 4):128 * (k - 3)],
                bv(x1l_o + 16 * k + 8 + 2 * j, 1), bv(idn_o, 128),
                start=(k % 4 == 0), stop=(k % 4 == 3)), waits=w,
                inc=(k == 7))
        pr.s(lambda s: s.copy(bv(wst_o, 512, 0, 1), pm6[0:1, 0:512]),
             waits=[("sT", tm)])
        s_er = pr.s(lambda s: s.copy(bv(wst_o + 512, 512, 0, 1), pm7[0:1, 0:512]),
                    inc=True)
        pr.t(lambda t: t.matmul(pm6[:, 0:512], bv(tri_o, 128, 0, 1), bv(wst_o, 512, 0, 1),
                                start=True, stop=True), waits=[("sS", s_er)])
        t_eb = pr.t(lambda t: t.matmul(pm7[:, 0:512], bv(tri_o, 128, 0, 1),
                                       bv(wst_o + 512, 512, 0, 1),
                                       start=True, stop=True), inc=True)
        pr.v(lambda v: v.tensor_scalar(bv(evb_o, 512), pm6[:, 0:512],
                                       fv(scf_o, 1), None, MUL),
             waits=[("sT", t_eb), ("sV", v_w2t)])
        v_eb = pr.v(lambda v: v.tensor_scalar(bv(evb_o + 512, 512),
                                              pm7[:, 0:512], fv(scf_o, 1),
                                              None, MUL), inc=True)
        for c2 in range(32):
            r2, kd = c2 // 8, c2 % 8
            oc = x1f_o + 16 * kd + 9 + 2 * j
            pr.v(lambda v, r2=r2, oc=oc: v.tensor_scalar(
                bv(sqs_o, 256).rearrange("p (a b) -> p a b", b=1),
                ab[:, evb_o:evb_o + 1024].rearrange("p (a b) -> p a b", b=4)
                [:, :, r2:r2 + 1],
                fv(oc, 1), None, MUL),
                waits=[("sV", v_eb)] if c2 == 0 else [], sync=True)
            v_w2eff = pr.v(lambda v, c2=c2, j=j: v.tensor_tensor(
                bv(w2t_o + 1024 * c2 + 256 * j, 256),
                bv(w2t_o + 1024 * c2 + 256 * j, 256), bv(sqs_o, 256), ADD),
                inc=True, sync=True)

    if upto <= 7:
        finish()

    v_ln2 = 0
    d_y_final = 0
    gelu_last = 0
    for fc in range(4):
        if upto <= 7 + fc:
            break
        F0 = ACH * fc
        d_x1c = 0
        for k in range(8):
            w = []
            if fc == 0 and k == 0:
                w = [("sT", t_p2), ("sV", v_p2), ("sS", s_p2)]
            elif k == 0:
                w = [("sS", gelu_last)]
            d_x1c = pr.ydma(lambda y, k=k, F0=F0: y.dma_start(
                bv(x1c_o + 512 * k, 512),
                x1t_h[:, 2048 * k + F0:2048 * k + F0 + 512]), waits=w)
        d_x1r = 0
        for tau in range(4):
            w = []
            if fc == 0 and tau == 0:
                w = [("sS", s_p2)]
            elif tau == 0:
                w = [("sV", v_ln2)]
            d_x1r = pr.ydma(lambda y, tau=tau, F0=F0: y.dma_start(
                fv(xc_o + 1024 * tau, 1024),
                x1sp_h[F0 + 128 * tau:F0 + 128 * (tau + 1), :]), waits=w)
        for mo in range(32):
            i, pm, c0 = next_half()
            for k in range(8):
                w = []
                if k == 0:
                    w = [("dY", d_x1c)]
                    if fc == 0 and mo == 0:
                        w.append(("sV", v_w1eff))
                    if hstate["ev"][i]:
                        w.append(("sS", hstate["ev"][i]))
                tm = pr.t(lambda t, pm=pm, c0=c0, k=k, mo=mo: t.matmul(
                    pm[:, c0:c0 + 512],
                    bv(w1t_o + 4096 * k + 128 * mo, 128),
                    bv(x1c_o + 512 * k, 512),
                    start=(k == 0), stop=(k == 7)), waits=w,
                    inc=(k == 7))
            w = [("sT", tm)]
            if fc == 0 and mo == 0:
                w += [("sS", s_p2), ("sV", v_p2)]
            gelu_last = pr.s(lambda s, pm=pm, c0=c0, mo=mo: s.activation(
                bv(hT_o + 512 * mo, 512), pm[:, c0:c0 + 512], AF.Gelu,
                bias=fv(b1s_o + mo, 1)), waits=w, inc=True)
            hstate["ev"][i] = gelu_last
        if dbg and fc == 0:
            pr.ydma(lambda y: y.dma_start(dbg_h[:, :], bv(hT_o, 16384)),
                    waits=[("sS", gelu_last)])
            pr.ydma(lambda y: y.dma_start(dbg_x1c[:, :], bv(x1c_o, 4096)),
                    waits=[("dY", d_x1c)])
            pr.ydma(lambda y: y.dma_start(dbg_w1t[:, :], bv(w1t_o, 32768)))
            pr.ydma(lambda y: y.dma_start(dbg_w2t[:, :], bv(w2t_o, 32768)))
        s_f2 = 0
        for mo in range(8):
            i, pm, c0 = next_half()
            for c2 in range(32):
                w = []
                if c2 == 0:
                    w = [("sS", gelu_last)]
                    if fc == 0 and mo == 0:
                        w.append(("sV", v_w2eff))
                    if hstate["ev"][i]:
                        w.append(("sS", hstate["ev"][i]))
                tm = pr.t(lambda t, pm=pm, c0=c0, c2=c2, mo=mo: t.matmul(
                    pm[:, c0:c0 + 512],
                    bv(w2t_o + 1024 * c2 + 128 * mo, 128),
                    bv(hT_o + 512 * c2, 512),
                    start=(c2 == 0), stop=(c2 == 31)), waits=w,
                    inc=(c2 == 31))
            w = [("sT", tm)]
            if fc == 0 and mo == 0:
                w.append(("dY", dY_p2))
            s_f2 = pr.s(lambda s, pm=pm, c0=c0, mo=mo: s.activation(
                bv(oTe_o + 512 * mo, 512), pm[:, c0:c0 + 512], AF.Identity,
                bias=fv(b2s_o + mo, 1)), waits=w, inc=True)
            hstate["ev"][i] = s_f2
        if dbg and fc == 0:
            pr.ydma(lambda y: y.dma_start(dbg_o[:, :], bv(oTe_o, 4096)),
                    waits=[("sS", s_f2)])
        v_st = 0
        for tl in range(4):
            for mh in range(2):
                pm = pm6 if (2 * tl + mh) % 2 == 0 else pm7
                for mi in range(4):
                    mo = 4 * mh + mi
                    w = []
                    if mi == 0:
                        w = [("sS", s_f2)] + pr.bar("sV")
                        if fc == 0 and tl == 0 and mh == 0:
                            w.append(("sT", t_wst_free[0]))
                    tm = pr.t(lambda t, pm=pm, mi=mi, mo=mo, tl=tl: t.matmul(
                        pm[:, 128 * mi:128 * (mi + 1)],
                        bv(oTe_o + 512 * mo + 128 * tl, 128), bv(idn_o, 128),
                        start=(mi == 0), stop=(mi == 3)), waits=w,
                        inc=(mi == 3))
                v_st = pr.v(lambda v, pm=pm, tl=tl, mh=mh:
                            v.scalar_tensor_tensor(
                                fv(xc_o + 1024 * tl + 512 * mh, 512),
                                pm[:, 0:512], 1.0,
                                fv(xc_o + 1024 * tl + 512 * mh, 512),
                                MUL, ADD,
                                accum_out=fv(st_o + 16 + 2 * tl + mh, 1)),
                            waits=[("sT", tm), ("dY", d_x1r)], inc=True)
        v_st = pr.v(lambda v: v.tensor_tensor(
            fv(st_o, 4),
            af[:, st_o + 16:st_o + 24].rearrange("p (a b) -> p a b", b=2)
            [:, :, 0:1],
            af[:, st_o + 16:st_o + 24].rearrange("p (a b) -> p a b", b=2)
            [:, :, 1:2], ADD), waits=[("sV", v_st)], inc=True, sync=True)
        s_sq4 = 0
        for tau in range(4):
            w = [("sV", v_st)]
            s_sq4 = pr.s(lambda s, tau=tau: s.activation(
                bv(evb_o, 1024), fv(xc_o + 1024 * tau, 1024), AF.Square,
                accum_out=fv(st_o + 4 + tau, 1)), waits=w, inc=True)
        pr.v(lambda v: v.tensor_scalar(fv(st_o + 8, 4), fv(st_o, 4),
                                       1.0 / D, None, MUL),
             waits=[("sS", s_sq4)], sync=True)
        pr.v(lambda v: v.tensor_tensor(fv(st_o, 4), fv(st_o + 8, 4),
                                       fv(st_o + 8, 4), MUL), sync=True)
        pr.v(lambda v: v.tensor_scalar(fv(st_o + 4, 4), fv(st_o + 4, 4),
                                       1.0 / D, None, MUL), sync=True)
        v_va = pr.v(lambda v: v.scalar_tensor_tensor(
            fv(st_o + 12, 4), fv(st_o + 4, 4), EPS, fv(st_o, 4), ADD, SUB),
            inc=True, sync=True)
        g_rs = pr.g(lambda g: g.tensor_tensor(fv(rs_o, 4), fv(st_o + 12, 4),
                                              fv(nh_o, 4), POW),
                    waits=[("sV", v_va)], inc=True)
        for tau in range(4):
            w = [("sG", g_rs)] if tau == 0 else []
            v_ln2 = pr.v(lambda v, tau=tau: v.tensor_scalar(
                fv(xc_o + 1024 * tau, 1024), fv(xc_o + 1024 * tau, 1024),
                fv(st_o + 8 + tau, 1), fv(rs_o + tau, 1), SUB, MUL),
                waits=w, inc=True, sync=True)
        for tau in range(4):
            d_y_final = pr.ydma(lambda y, F0=F0, tau=tau: y.dma_start(
                y_h[F0 + 128 * tau:F0 + 128 * (tau + 1), :],
                fv(xc_o + 1024 * tau, 1024)), waits=[("sV", v_ln2)])

    if upto > 10:
        pr.op("Y", lambda y: y.nop(), [("dY", d_y_final)], ())
    else:
        finish()

    with (
        nc.semaphore("hsT") as hsT,
        nc.semaphore("hsV") as hsV,
        nc.semaphore("hsS") as hsS,
        nc.semaphore("hsG") as hsG,
        nc.semaphore("hdG") as hdG,
        nc.semaphore("hdY") as hdY,
        nc.Block() as blk,
    ):
        smap = {"sT": hsT, "sV": hsV, "sS": hsS, "sG": hsG,
                "dG": hdG, "dY": hdY}

        def replay(eng_key):
            def inner(eng):
                for fn, waits, incs in pr.ops[eng_key]:
                    for s, vv in waits:
                        eng.wait_ge(smap[s], vv)
                    ins = fn(eng)
                    for s, n in incs:
                        ins.then_inc(smap[s], n)
            return inner

        blk.tensor(replay("T"))
        blk.vector(replay("V"))
        blk.scalar(replay("S"))
        blk.gpsimd(replay("G"))
        blk.sync(replay("Y"))
    return nc


_NC_CACHE = {}


def kernel(**inputs):
    x = np.ascontiguousarray(np.asarray(inputs["L0_attn_x"]))
    wc = np.ascontiguousarray(np.asarray(inputs["L0_attn_Wc_down"]))
    wq = np.ascontiguousarray(np.asarray(inputs["L0_attn_Wq_up"]))
    wkv = np.ascontiguousarray(np.asarray(inputs["L0_attn_Wkv_up"]))
    wp = np.ascontiguousarray(np.asarray(inputs["L0_attn_Wc_proj"]))
    w1 = np.ascontiguousarray(np.asarray(inputs["L0_ffn_W1"]))
    b1 = np.ascontiguousarray(np.asarray(inputs["L0_ffn_b1"]))
    w2 = np.ascontiguousarray(np.asarray(inputs["L0_ffn_W2"]))
    b2 = np.ascontiguousarray(np.asarray(inputs["L0_ffn_b2"]))
    scl = np.ascontiguousarray(np.asarray(inputs["L0_ffn_scale"],
                                          dtype=np.float32).reshape(1, 1))
    if "nc" not in _NC_CACHE:
        _NC_CACHE["nc"] = build_program(paranoid="VSG:dG,dY")
    nc = _NC_CACHE["nc"]
    in_maps = [
        {"x": x[i], "wc": wc, "wq": wq, "wkv": wkv, "wp": wp,
         "w1": w1, "b1": b1, "w2": w2, "b2": b2, "scl": scl}
        for i in range(8)
    ]
    res = run_bass_kernel_spmd(nc, in_maps, core_ids=list(range(8)))
    return np.stack([np.asarray(res.results[i]["y"]) for i in range(8)],
                    axis=0)


# version 07 (final)
# speedup vs baseline: 1.3753x; 1.1134x over previous
import numpy as np
import concourse.bass as bass
import concourse.mybir as mybir
from concourse.alu_op_type import AluOpType
from concourse.bass_utils import run_bass_kernel_spmd

F32 = mybir.dt.float32
BF16 = mybir.dt.bfloat16
I32 = mybir.dt.int32
AF = mybir.ActivationFunctionType
AXX = mybir.AxisListType.X
ADD, MUL, SUB = AluOpType.add, AluOpType.mult, AluOpType.subtract
POW, ISGE, ISEQ = AluOpType.pow, AluOpType.is_ge, AluOpType.is_equal

T, D, HID, LAT, NH, HD = 2048, 1024, 4096, 512, 16, 64
S8 = float(8.0 ** -0.5)
EPS = 1e-5
ACH = 512
ENGS = ("T", "V", "S", "G", "Y")


PARANOID = [False]


class P:
    def __init__(self):
        self.ops = {e: [] for e in ENGS}
        self.cnt = {"sT": 0, "sV": 0, "sS": 0, "sG": 0, "dG": 0, "dY": 0}
        self.done = False

    def op(self, eng, fn, waits=(), incs=()):
        if self.done:
            return self.cnt
        if PARANOID[0] and eng in PARANOID[0].split(":")[0]:
            sems = PARANOID[0].split(":")[1] if ":" in PARANOID[0] else "sT,sV,sS,sG,dG,dY"
            waits = list(waits) + [(s, self.cnt[s]) for s in sems.split(",")]
        ws = tuple((s, v) for s, v in waits if v > 0)
        self.ops[eng].append((fn, ws, tuple(incs)))
        for s, n in incs:
            self.cnt[s] += n
        return self.cnt

    def t(self, fn, waits=(), inc=False):
        return self.op("T", fn, waits, (("sT", 1),) if inc else ())["sT"]

    def v(self, fn, waits=(), inc=False, sync=False):
        if sync:
            waits = list(waits) + [("sV", max(1, self.cnt["sV"]))]
        if not inc:
            return self.op("V", fn, waits, ())["sV"]
        self.op("V", fn, waits, ())
        return self.op("V", lambda e: e.drain(), (), (("sV", 1),))["sV"]

    def s(self, fn, waits=(), inc=False):
        if not inc:
            return self.op("S", fn, waits, ())["sS"]
        self.op("S", fn, waits, ())
        return self.op("S", lambda e: e.drain(), (), (("sS", 1),))["sS"]

    def g(self, fn, waits=(), inc=False):
        if not inc:
            return self.op("G", fn, waits, ())["sG"]
        self.op("G", fn, waits, ())
        return self.op("G", lambda e: e.drain(), (), (("sG", 1),))["sG"]

    def gdma(self, fn, waits=()):
        return self.op("G", fn, waits, (("dG", 16),))["dG"]

    def ydma(self, fn, waits=()):
        return self.op("Y", fn, waits, (("dY", 16),))["dY"]

    def m(self, s):
        return self.cnt[s]

    def bar(self, *sems):
        return [(s, self.cnt[s]) for s in sems if self.cnt[s] > 0]


def build_program(dbg=False, upto=99, paranoid=""):
    PARANOID[0] = paranoid
    nc = bass.Bass("TRN2", target_bir_lowering=False,
                   detect_race_conditions=False)

    x_h = nc.dram_tensor("x", [T, D], F32, kind="ExternalInput").ap()
    wc_h = nc.dram_tensor("wc", [D, LAT], F32, kind="ExternalInput").ap()
    wq_h = nc.dram_tensor("wq", [LAT, D], F32, kind="ExternalInput").ap()
    wkv_h = nc.dram_tensor("wkv", [LAT, 2 * HD], F32, kind="ExternalInput").ap()
    wp_h = nc.dram_tensor("wp", [D, D], F32, kind="ExternalInput").ap()
    w1_h = nc.dram_tensor("w1", [HID, D], F32, kind="ExternalInput").ap()
    b1_h = nc.dram_tensor("b1", [HID], F32, kind="ExternalInput").ap()
    w2_h = nc.dram_tensor("w2", [D, HID], F32, kind="ExternalInput").ap()
    b2_h = nc.dram_tensor("b2", [D], F32, kind="ExternalInput").ap()
    scl_h = nc.dram_tensor("scl", [1, 1], F32, kind="ExternalInput").ap()
    y_h = nc.dram_tensor("y", [T, D], F32, kind="ExternalOutput").ap()
    x1sp_h = nc.dram_tensor("x1sp", [T, D], F32).ap()
    if dbg:
        dbg_h = nc.dram_tensor("dbg_h", [128, 16384], BF16,
                               kind="ExternalOutput").ap()
        dbg_o = nc.dram_tensor("dbg_o", [128, 4096], BF16,
                               kind="ExternalOutput").ap()
        dbg_x1c = nc.dram_tensor("dbg_x1c", [128, 4096], BF16,
                                 kind="ExternalOutput").ap()
        dbg_w1t = nc.dram_tensor("dbg_w1t", [128, 32768], BF16,
                                 kind="ExternalOutput").ap()
        dbg_w2t = nc.dram_tensor("dbg_w2t", [128, 32768], BF16,
                                 kind="ExternalOutput").ap()
        dbg_x1 = nc.dram_tensor("dbg_x1", [T, D], F32,
                                kind="ExternalOutput").ap()
        dbg_an = nc.dram_tensor("dbg_an", [T, 4096], BF16,
                                kind="ExternalOutput").ap()
        dbg_yn = nc.dram_tensor("dbg_yn", [128, 4 * 4096], BF16,
                                kind="ExternalOutput").ap()
    x1t_h = nc.dram_tensor("x1t", [128, 8 * T], BF16).ap()

    w1t_o = 0
    xT_o, latT_o, wc_o, wq_o = 0, 16384, 24576, 28672
    w2t_o = 32768
    qT_o, aevT_o, anat_o, x1nb_o = 32768, 49152, 53248, 57344
    C0 = 65536
    wp_o = C0
    kv_o = C0 + 8192
    vex_o = C0 + 10240
    yTn_o = C0 + 11296
    att_o = C0 + 15392
    ynat_o = C0 + 19488
    x1Ts_o = C0 + 20000
    sqs_o = C0 + 24096
    qsq_o = sqs_o
    wst_o = C0 + 25120
    evb_o = C0 + 29216
    hT_o = C0
    x1c_o = C0 + 16384
    oTe_o = C0 + 20480
    sm_o = C0 + 30240
    idn_o = sm_o
    tri_o = sm_o + 128
    sel_o = sm_o + 256
    x1l_o = sm_o + 384
    one_o = sm_o + 512
    NB = sm_o + 520
    ab = nc.alloc_sbuf_tensor("ab", [128, NB], BF16)

    xc_o = 0
    st_o = 4096
    ncm_o = 4120
    x1f_o = 4136
    b1s_o = 4264
    b2s_o = 4296
    scf_o = 4304
    nh_o = 4305
    rs_o = 4309
    rc_o = 4313
    mx_o = 4317
    NF = 4341
    af = nc.alloc_sbuf_tensor("af", [128, NF], F32)

    def bv(off, n, p0=0, p1=128):
        return ab[p0:p1, off:off + n]

    def fv(off, n, p0=0, p1=128):
        return af[p0:p1, off:off + n]

    iotat = nc.alloc_sbuf_tensor("iotat", [128, 128], I32)
    identf = nc.alloc_sbuf_tensor("identf", [32, 32], F32)
    ytme = nc.alloc_sbuf_tensor("ytme", [65, 512], BF16)
    ytmo = nc.alloc_sbuf_tensor("ytmo", [65, 512], BF16)
    sclna = nc.alloc_sbuf_tensor("sclna", [1, 1], F32)
    mrow = nc.alloc_sbuf_tensor("mrow", [1, 16], F32)
    mrowb = nc.alloc_sbuf_tensor("mrowb", [1, 16], BF16)

    pd_e = nc.alloc_psum_tensor("pd_e", [128, 1024], F32)
    pd_o = nc.alloc_psum_tensor("pd_o", [128, 1024], F32)
    py_e = nc.alloc_psum_tensor("py_e", [128, 512], F32)
    py_o = nc.alloc_psum_tensor("py_o", [128, 512], F32)
    pm6 = nc.alloc_psum_tensor("pm6", [128, 512], F32)
    pm7 = nc.alloc_psum_tensor("pm7", [128, 512], F32)

    pr = P()

    def finish():
        pr.op("Y", lambda y: y.nop(),
              [(s, pr.m(s)) for s in ("sT", "sV", "sS", "sG", "dG", "dY")], ())
        pr.done = True


    g_io = pr.g(lambda g: g.iota(iotat[:], [[1, 128]], channel_multiplier=-1,
                                 allow_small_or_imprecise_dtypes=True), inc=True)
    pr.v(lambda v: v.tensor_scalar(bv(idn_o, 128), iotat[:], 0.0, None, ISEQ),
         waits=[("sG", g_io)])
    pr.v(lambda v: v.tensor_scalar(bv(tri_o, 128), iotat[:], 0.0, None, ISGE))
    pr.v(lambda v: v.tensor_scalar(identf[:], iotat[0:32, 0:32], 0.0, None, ISEQ))
    pr.v(lambda v: v.memset(bv(sel_o, 128), 0.0))
    for k in range(8):
        for h in (2 * k, 2 * k + 1):
            pr.v(lambda v, k=k, h=h: v.memset(
                bv(sel_o + 16 * k + h, 1, 64 * (h % 2), 64 * (h % 2) + 64), 1.0))
    pr.v(lambda v: v.memset(bv(one_o, 1), 1.0))
    pr.v(lambda v: v.memset(fv(nh_o, 4), -0.5))
    v_cn = pr.v(lambda v: v.memset(
        ab[:, vex_o:vex_o + 1040].rearrange("p (j c) -> p j c", c=65)[:, :, 64:65],
        1.0), inc=True)

    d_wc = 0
    for k in range(8):
        d_wc = pr.gdma(lambda g, k=k: g.dma_start(
            bv(wc_o + 512 * k, 512), wc_h[128 * k:128 * (k + 1), :]))
    for k in range(4):
        d_wq = pr.gdma(lambda g, k=k: g.dma_start(
            bv(wq_o + 1024 * k, 1024), wq_h[128 * k:128 * (k + 1), :]))
    for k in range(4):
        d_wkv = pr.gdma(lambda g, k=k: g.dma_start(
            bv(qsq_o + 128 * k, 128), wkv_h[128 * k:128 * (k + 1), :]))
    for k in range(8):
        d_wp = pr.gdma(lambda g, k=k: g.dma_start(
            bv(wp_o + 1024 * k, 1024), wp_h[128 * k:128 * (k + 1), :]))
    d_b1 = pr.gdma(lambda g: g.dma_start(
        fv(xc_o, 128, 0, 32), b1_h.rearrange("(m r) -> m r", r=128)))
    d_b2 = pr.gdma(lambda g: g.dma_start(
        fv(xc_o + 128, 128, 0, 8), b2_h.rearrange("(m r) -> m r", r=128)))
    d_sc = pr.gdma(lambda g: g.dma_start(sclna[:], scl_h[:, :]))

    pr.t(lambda t: t.transpose(pm6[0:128, 0:32], fv(xc_o, 128, 0, 32), identf[:]),
         waits=[("dG", d_b2), ("sV", v_cn)])
    t_bt = pr.t(lambda t: t.transpose(pm6[0:128, 32:40], fv(xc_o + 128, 128, 0, 8),
                                      identf[0:8, 0:8]), inc=True)
    pr.s(lambda s: s.copy(fv(b1s_o, 32), pm6[:, 0:32]), waits=[("sT", t_bt)])
    s_b = pr.s(lambda s: s.copy(fv(b2s_o, 8), pm6[:, 32:40]), inc=True)
    v_sc = pr.v(lambda v: v.tensor_copy(mrowb[:, 0:1], sclna[:]),
                waits=[("dG", d_sc)], inc=True)
    t_sc2 = pr.t(lambda t: t.matmul(pm6[:, 40:41], bv(tri_o, 128, 0, 1), mrowb[0:1, 0:1],
                                    start=True, stop=True),
                 waits=[("sV", v_sc), ("sS", s_b)], inc=True)
    s_scl = pr.s(lambda s: s.copy(fv(scf_o, 1), pm6[:, 40:41]),
                 waits=[("sT", t_sc2)], inc=True)

    if upto <= 0.2:
        finish()
    xst_o = x1Ts_o
    s_xT = 0
    t_xgrp = 0
    for gi in range(4):
        d_x = 0
        for tau in range(4):
            w = [("sT", t_xgrp)] if tau == 0 else []
            d_x = pr.gdma(lambda g, gi=gi, tau=tau: g.dma_start(
                bv(xst_o + 1024 * tau, 1024),
                x_h[512 * gi + 128 * tau:512 * gi + 128 * (tau + 1), :]),
                waits=w)
        for k in range(8):
            pm = pm6 if (gi * 8 + k) % 2 == 0 else pm7
            for tau in range(4):
                w = []
                if tau == 0:
                    w = [("dG", d_x)] + pr.bar("sS")
                tm = pr.t(lambda t, pm=pm, tau=tau, k=k: t.matmul(
                    pm[:, 128 * tau:128 * (tau + 1)],
                    bv(xst_o + 1024 * tau + 128 * k, 128), bv(idn_o, 128),
                    start=(tau == 0), stop=(tau == 3)), waits=w,
                    inc=(tau == 3))
            s_xT = pr.s(lambda s, pm=pm, k=k, gi=gi: s.copy(
                bv(xT_o + 2048 * k + 512 * gi, 512), pm[:, 0:512]),
                waits=[("sT", tm)], inc=True)
        t_xgrp = pr.m("sT")

    if upto <= 0.4:
        finish()
    halves = [(pd_e, 0), (pd_e, 512), (pd_o, 0), (pd_o, 512)]
    hstate = {"i": 0, "ev": [0, 0, 0, 0]}

    def next_half():
        i = hstate["i"] % 4
        hstate["i"] += 1
        return i, halves[i][0], halves[i][1]

    def mm_chain(lhs_fn, rhs_fn, nk, evac_fn, extra_waits=()):
        i, pm, c0 = next_half()
        tm = 0
        for k in range(nk):
            w = list(extra_waits) if k == 0 else []
            if k == 0 and hstate["ev"][i]:
                w.append(("sS", hstate["ev"][i]))
            tm = pr.t(lambda t, pm=pm, c0=c0, k=k: t.matmul(
                pm[:, c0:c0 + 512], lhs_fn(k), rhs_fn(k),
                start=(k == 0), stop=(k == nk - 1)), waits=w,
                inc=(k == nk - 1))
        mk = pr.s(lambda s, pm=pm, c0=c0: evac_fn(s, pm, c0),
                  waits=[("sT", tm)], inc=True)
        hstate["ev"][i] = mk
        return mk

    s_lat = 0
    for m in range(4):
        for tc in range(4):
            ew = [("dG", d_wc), ("sS", s_xT)] if (m == 0 and tc == 0) else []
            s_lat = mm_chain(
                lambda k, m=m: bv(wc_o + 512 * k + 128 * m, 128),
                lambda k, tc=tc: bv(xT_o + 2048 * k + 512 * tc, 512), 8,
                lambda s, pm, c0, m=m, tc=tc: s.copy(
                    bv(latT_o + 2048 * m + 512 * tc, 512), pm[:, c0:c0 + 512]),
                ew)
    for m in range(8):
        for tc in range(4):
            ew = [("dG", d_wq), ("sS", s_lat)] if (m == 0 and tc == 0) else []
            mm_chain(
                lambda k, m=m: bv(wq_o + 1024 * k + 128 * m, 128),
                lambda k, tc=tc: bv(latT_o + 2048 * k + 512 * tc, 512), 4,
                lambda s, pm, c0, m=m, tc=tc: s.activation(
                    bv(qT_o + 2048 * m + 512 * tc, 512), pm[:, c0:c0 + 512],
                    AF.Copy, scale=S8), ew)
    if upto <= 0.6:
        finish()
    s_kv = 0
    for tc in range(4):
        iA, pmA, cA = next_half()
        iB, pmB, cB = next_half()
        for k in range(4):
            w = []
            if k == 0:
                w = [("dG", d_wkv), ("sS", s_lat)]
                if hstate["ev"][iA]:
                    w.append(("sS", hstate["ev"][iA]))
            pr.t(lambda t, pmA=pmA, cA=cA, k=k, tc=tc: t.matmul(
                pmA[:, cA:cA + 512], bv(qsq_o + 128 * k, 128),
                bv(latT_o + 2048 * k + 512 * tc, 512),
                start=(k == 0), stop=(k == 3)), waits=w)
        for k in range(4):
            w = []
            if k == 0 and hstate["ev"][iB]:
                w = [("sS", hstate["ev"][iB])]
            tm = pr.t(lambda t, pmB=pmB, cB=cB, k=k, tc=tc: t.matmul(
                pmB[64:128, cB:cB + 512], bv(qsq_o + 128 * k, 64),
                bv(latT_o + 2048 * k + 512 * tc, 512),
                start=(k == 0), stop=(k == 3), tile_position=(0, 64)), waits=w,
                inc=(k == 3))
        pr.s(lambda s, pmA=pmA, cA=cA, tc=tc: s.activation(
            bv(kv_o + 512 * tc, 512, 0, 64), pmA[0:64, cA:cA + 512],
            AF.Copy, scale=S8), waits=[("sT", tm)])
        pr.s(lambda s, pmB=pmB, cB=cB, tc=tc: s.activation(
            bv(kv_o + 512 * tc, 512, 64, 128), pmB[64:128, cB:cB + 512],
            AF.Copy, scale=S8))
        s_kv = pr.s(lambda s, pmA=pmA, cA=cA, tc=tc: s.copy(
            bv(att_o + 512 * tc, 512, 64, 128), pmA[64:128, cA:cA + 512]),
            inc=True)
        hstate["ev"][iA] = s_kv
        hstate["ev"][iB] = s_kv
    if upto <= 0.7:
        finish()
    s_vx = 0
    for jg in range(4):
        pm = pm6 if jg % 2 == 0 else pm7
        for ji in range(4):
            j = 4 * jg + ji
            w = []
            if ji == 0:
                w = [("sS", s_kv)] + pr.bar("sS")
            tm = pr.t(lambda t, pm=pm, ji=ji, j=j: t.matmul(
                pm[:, 64 * ji:64 * (ji + 1)],
                bv(att_o + 128 * j, 128, 64, 128),
                bv(idn_o + 64, 64, 64, 128),
                start=(ji == 0), stop=(ji == 3), tile_position=(64, 0)),
                waits=w, inc=(ji == 3))
        s_vx = pr.s(lambda s, pm=pm, jg=jg: s.copy(
            ab[:, vex_o + 260 * jg:vex_o + 260 * (jg + 1)]
            .rearrange("p (j c) -> p j c", c=65)[:, :, 0:64],
            pm[:, 0:256].rearrange("p (j c) -> p j c", c=64)),
            waits=[("sT", tm)], inc=True)

    if upto <= 0.8:
        finish()
    t_sqmm = {}
    for tc in range(4):
        pm, c0 = [(pd_e, 0), (pd_e, 512), (pd_o, 0), (pd_o, 512)][tc]
        for k in range(8):
            w = [("sS", s_vx)] if (tc == 0 and k == 0) else []
            if (tc, k - 2) in t_sqmm:
                w.append(("sT", t_sqmm[(tc, k - 2)]))
            s_sq = pr.s(lambda s, k=k, tc=tc: s.activation(
                bv(sqs_o + 512 * (k % 2), 512),
                bv(qT_o + 2048 * k + 512 * tc, 512), AF.Square),
                waits=w, inc=True)
            t_sqmm[(tc, k)] = pr.t(lambda t, pm=pm, c0=c0, k=k: t.matmul(
                pm[0:16, c0:c0 + 512], bv(sel_o + 16 * k, 16),
                bv(sqs_o + 512 * (k % 2), 512),
                start=(k == 0), stop=(k == 7)),
                waits=[("sS", s_sq)], inc=True)
        s_sq = pr.s(lambda s, tc=tc: s.activation(
            bv(sqs_o, 512, 0, 64), bv(kv_o + 512 * tc, 512, 0, 64),
            AF.Square), waits=[("sT", t_sqmm[(tc, 7)])], inc=True)
        tm = pr.t(lambda t, tc=tc: t.matmul(
            pm6[0:1, 128 * tc:128 * (tc + 1)], bv(one_o, 1, 0, 64),
            bv(sqs_o, 128, 0, 64), start=(tc == 0), stop=(tc == 3)),
            waits=[("sS", s_sq)], inc=(tc == 3))
    for tc in range(4):
        pm, c0 = [(pd_e, 0), (pd_e, 512), (pd_o, 0), (pd_o, 512)][tc]
        pr.v(lambda v, pm=pm, c0=c0, tc=tc: v.reduce_max(
            fv(mx_o + tc, 1, 0, 16), pm[0:16, c0:c0 + 512], axis=AXX),
            waits=[("sT", tm)] if tc == 0 else [])
    pr.v(lambda v: v.reduce_max(fv(mx_o + 4, 1, 0, 16),
                                fv(mx_o, 4, 0, 16), axis=AXX))
    v_mx = pr.v(lambda v: v.reduce_max(fv(mx_o + 5, 1, 0, 1),
                                       pm6[0:1, 0:512], axis=AXX), inc=True)
    v_of = pr.v(lambda v: v.memset(fv(mx_o + 6, 16, 0, 1), 1.0), inc=True)
    pr.t(lambda t: t.transpose(pm7[0:1, 0:16], fv(mx_o + 4, 1, 0, 16),
                               identf[0:16, 0:16]),
         waits=[("sV", v_mx), ("sV", v_of)] + pr.bar("sS"))
    t_mx = pr.t(lambda t: t.matmul(pm7[0:1, 0:16], fv(mx_o + 5, 1, 0, 1),
                                   fv(mx_o + 6, 16, 0, 1),
                                   start=False, stop=True,
                                   skip_group_check=True), inc=True)
    s_mx = pr.s(lambda s: s.copy(mrow[:, 0:16], pm7[0:1, 0:16]),
                waits=[("sT", t_mx)], inc=True)
    v_nc = pr.v(lambda v: v.tensor_copy(mrowb[:, 0:16], mrow[:, 0:16]),
                waits=[("sS", s_mx)], inc=True)
    t_nc = pr.t(lambda t: t.matmul(pm6[:, 64:80], bv(tri_o, 128, 0, 1),
                                   mrowb[0:1, 0:16],
                                   start=True, stop=True),
                waits=[("sV", v_nc), ("sS", s_mx)], inc=True)
    s_ncm = pr.s(lambda s: s.activation(fv(ncm_o, 16), pm6[:, 64:80],
                                        AF.Copy, scale=-0.5),
                 waits=[("sT", t_nc)], inc=True)

    if upto <= 1:
        if dbg:
            for k in range(8):
                pr.ydma(lambda y, k=k: y.dma_start(
                    dbg_h[:, 2048 * k:2048 * (k + 1)],
                    bv(qT_o + 2048 * k, 2048)),
                    waits=pr.bar("sS", "sV", "sT", "sG"))
            for k in range(4):
                pr.ydma(lambda y, k=k: y.dma_start(
                    dbg_yn[:, 2048 * k:2048 * (k + 1)],
                    bv(latT_o + 2048 * k, 2048)))
            pr.ydma(lambda y: y.dma_start(dbg_yn[:, 8192:10240],
                                          bv(kv_o, 2048)))
            pr.ydma(lambda y: y.dma_start(dbg_yn[:, 10240:11280],
                                          bv(vex_o, 1040)))
            pr.ydma(lambda y: y.dma_start(
                dbg_x1.rearrange("a b -> (a b)")[0:128 * 4341]
                .rearrange("(p q) -> p q", q=4341), fv(0, 4341)))
        finish()

    t_wst_free = [0]

    def emit_w1_group(rg, ev_marks):
        dm = 0
        for ri in range(4):
            r = 4 * rg + ri
            w = [("sT", t_wst_free[0])] if ri == 0 else []
            dm = pr.gdma(lambda g, r=r, ri=ri: g.dma_start(
                bv(wst_o + 1024 * ri, 1024), w1_h[128 * r:128 * (r + 1), :]),
                waits=w)
        for k in range(8):
            pm = pm6 if k % 2 == 0 else pm7
            for ri in range(4):
                w = []
                if ri == 0:
                    w = pr.bar("sV", "sS")
                    if k == 0:
                        w.append(("dG", dm))
                tm2 = pr.t(lambda t, pm=pm, ri=ri, k=k: t.matmul(
                    pm[:, 128 * ri:128 * (ri + 1)],
                    bv(wst_o + 1024 * ri + 128 * k, 128), bv(idn_o, 128),
                    start=(ri == 0), stop=(ri == 3)), waits=w,
                    inc=(ri == 3))
            if k == 7:
                t_wst_free[0] = tm2
            mk = pr.v(lambda v, pm=pm, k=k, rg=rg: v.tensor_copy(
                bv(w1t_o + 4096 * k + 512 * rg, 512), pm[:, 0:512]),
                waits=[("sT", tm2)], inc=True)
            ev_marks.append(mk)

    w1_ev = []

    s_exp = 0
    t_att_hist = []
    v_yTn = 0
    s_proj = 0
    v_ln1 = 0
    s_cv_prev = 0
    dY_x1T_prev = 0
    v_w1eff = 0

    for c in range(4):
        if upto <= 1 + c:
            break
        T0 = ACH * c
        jmax = 4 * c + 3
        d_xc = 0
        for tau in range(4):
            w = []
            if tau == 0 and c > 0:
                w = [("sV", v_ln1), ("sS", s_cv_prev)]
            elif tau == 0:
                w = [("sS", s_scl)]
            d_xc = pr.ydma(lambda y, T0=T0, tau=tau: y.dma_start(
                fv(xc_o + 1024 * tau, 1024),
                x_h[T0 + 128 * tau:T0 + 128 * (tau + 1), :]), waits=w)

        for p in range(8):
            js = list(range(jmax + 1))
            duos = [js[i:i + 2] for i in range(0, len(js), 2)]
            for di, duo in enumerate(duos):
                ab_off = att_o + 1024 * (di % 2)
                for idx, j in enumerate(duo):
                    coff = max(0, 128 * j - T0)
                    cols = ACH - coff
                    w = []
                    if idx == 0:
                        w = [("sS", s_exp)]
                        if s_proj:
                            w.append(("sS", s_proj))
                        if c == 0 and p == 0 and di == 0:
                            w += [("sS", s_ncm), ("sS", s_vx)]
                    pr.t(lambda t, j=j, idx=idx, coff=coff, cols=cols, p=p,
                         T0=T0: t.matmul(
                             pd_e[:, 512 * idx:512 * idx + cols],
                             bv(kv_o + 128 * j, 128, 0, 64),
                             bv(qT_o + 2048 * p + T0 + coff, cols, 0, 64),
                             start=True, stop=True, tile_position=(0, 0)),
                         waits=w)
                    t_sc = pr.t(lambda t, j=j, idx=idx, coff=coff, cols=cols,
                                p=p, T0=T0: t.matmul(
                                    pd_o[:, 512 * idx:512 * idx + cols],
                                    bv(kv_o + 128 * j, 128, 64, 128),
                                    bv(qT_o + 2048 * p + T0 + coff, cols,
                                       64, 128),
                                    start=True, stop=True,
                                    tile_position=(64, 0)),
                                inc=(idx == len(duo) - 1))
                span = 512 * (len(duo) - 1) + ACH - max(0, 128 * duo[-1] - T0)
                we = [("sT", t_sc)]
                if len(t_att_hist) >= 2:
                    we.append(("sT", t_att_hist[-2]))
                pr.s(lambda s, span=span, p=p, ab_off=ab_off: s.activation(
                    bv(ab_off, span), pd_e[:, 0:span], AF.Exp,
                    bias=fv(ncm_o + 2 * p, 1)), waits=we)
                s_exp = pr.s(lambda s, span=span, p=p, ab_off=ab_off:
                             s.activation(bv(ab_off + 2048, span),
                                          pd_o[:, 0:span], AF.Exp,
                                          bias=fv(ncm_o + 2 * p + 1, 1)),
                             inc=True)
                v_mask = 0
                for idx, j in enumerate(duo):
                    if 128 * j >= T0:
                        pr.v(lambda v, idx=idx, ab_off=ab_off: v.tensor_tensor(
                            bv(ab_off + 512 * idx, 128),
                            bv(ab_off + 512 * idx, 128), bv(tri_o, 128), MUL),
                            waits=[("sS", s_exp)])
                        v_mask = pr.v(lambda v, idx=idx, ab_off=ab_off:
                                      v.tensor_tensor(
                                          bv(ab_off + 2048 + 512 * idx, 128),
                                          bv(ab_off + 2048 + 512 * idx, 128),
                                          bv(tri_o, 128), MUL), inc=True)
                for idx, j in enumerate(duo):
                    coff = max(0, 128 * j - T0)
                    cols = ACH - coff
                    w = []
                    if idx == 0:
                        w = [("sS", s_exp)]
                        if v_mask:
                            w.append(("sV", v_mask))
                        if j == 0 and v_yTn:
                            w.append(("sV", v_yTn))
                    pr.t(lambda t, j=j, idx=idx, coff=coff, cols=cols,
                         jmax=jmax, ab_off=ab_off: t.matmul(
                             py_e[0:65, coff:coff + cols],
                             ab[:, vex_o + 65 * j:vex_o + 65 * j + 65],
                             bv(ab_off + 512 * idx, cols),
                             start=(j == 0), stop=(j == jmax)), waits=w)
                    t_av2 = pr.t(lambda t, j=j, idx=idx, coff=coff, cols=cols,
                                 jmax=jmax, ab_off=ab_off: t.matmul(
                                     py_o[0:65, coff:coff + cols],
                                     ab[:, vex_o + 65 * j:vex_o + 65 * j + 65],
                                     bv(ab_off + 2048 + 512 * idx, cols),
                                     start=(j == 0), stop=(j == jmax)),
                                 inc=(idx == len(duo) - 1))
                t_att_hist.append(t_av2)

            t_av = t_att_hist[-1]
            pr.v(lambda v: v.tensor_copy(ytme[:, :], py_e[0:65, :]),
                 waits=[("sT", t_av)])
            v_yt = pr.v(lambda v: v.tensor_copy(ytmo[:, :], py_o[0:65, :]),
                        inc=True)
            for ytmp, pmF in ((ytme, pm6), (ytmo, pm7)):
                for tau in range(4):
                    w = []
                    if tau == 0:
                        w = [("sV", v_yt)] + pr.bar("sS")
                    t_fwd = pr.t(lambda t, ytmp=ytmp, pmF=pmF, tau=tau:
                                 t.matmul(
                                     pmF[:, 65 * tau:65 * (tau + 1)],
                                     ytmp[0:65, 128 * tau:128 * (tau + 1)],
                                     bv(idn_o, 65, 0, 65),
                                     start=(tau == 0), stop=(tau == 3)),
                                 waits=w,
                                 inc=(tau == 3 and pmF is pm7))
            t_bk = 0
            for hh, pmF in ((0, pm6), (1, pm7)):
                yo = ynat_o + 256 * hh
                pr.v(lambda v, pmF=pmF: v.reciprocal(
                    fv(rc_o, 4).rearrange("p (a b) -> p a b", b=1),
                    pmF[:, 0:260].rearrange("p (t c) -> p t c", c=65)
                    [:, :, 64:65]), waits=[("sT", t_fwd)])
                v_nr = 0
                for tau in range(4):
                    v_nr = pr.v(lambda v, pmF=pmF, tau=tau, yo=yo:
                                v.tensor_scalar(
                                    bv(yo + 64 * tau, 64),
                                    pmF[:, 65 * tau:65 * tau + 64],
                                    fv(rc_o + tau, 1), None, MUL), inc=True,
                                sync=(tau == 0))
                for tau in range(4):
                    w = [("sV", v_nr)] if tau == 0 else []
                    t_bk = pr.t(lambda t, tau=tau, hh=hh, yo=yo: t.matmul(
                        py_e[64 * hh:64 * hh + 64, 128 * tau:128 * (tau + 1)],
                        bv(yo + 64 * tau, 64), bv(idn_o, 128),
                        start=(tau == 0), stop=(tau == 3),
                        tile_position=(0, 64 * hh)), waits=w,
                        inc=(tau == 3))
            v_yTn = pr.v(lambda v, p=p: v.tensor_copy(
                bv(yTn_o + 512 * p, 512), py_e[:, 0:512]),
                waits=[("sT", t_bk)], inc=True)

            if c == 0:
                emit_w1_group(p, w1_ev)

        for mo in range(8):
            i, pm, c0 = next_half()
            for k in range(8):
                w = []
                if k == 0:
                    w = [("sV", v_yTn), ("sS", s_exp)]
                    if mo == 0:
                        w.append(("dG", d_wp))
                    if hstate["ev"][i]:
                        w.append(("sS", hstate["ev"][i]))
                tm = pr.t(lambda t, pm=pm, c0=c0, k=k, mo=mo: t.matmul(
                    pm[:, c0:c0 + 512],
                    bv(wp_o + 1024 * k + 128 * mo, 128),
                    bv(yTn_o + 512 * k, 512),
                    start=(k == 0), stop=(k == 7)), waits=w,
                    inc=(k == 7))
            s_proj = pr.s(lambda s, pm=pm, c0=c0, mo=mo: s.copy(
                bv(aevT_o + 512 * mo, 512), pm[:, c0:c0 + 512]),
                waits=[("sT", tm)], inc=True)
            hstate["ev"][i] = s_proj
        v_an = 0
        for tl in range(4):
            for mh in range(2):
                pm = pm6 if (2 * tl + mh) % 2 == 0 else pm7
                for mi in range(4):
                    mo = 4 * mh + mi
                    w = []
                    if mi == 0:
                        w = [("sS", s_proj)] + pr.bar("sV")
                    tm = pr.t(lambda t, pm=pm, mi=mi, mo=mo, tl=tl: t.matmul(
                        pm[:, 128 * mi:128 * (mi + 1)],
                        bv(aevT_o + 512 * mo + 128 * tl, 128), bv(idn_o, 128),
                        start=(mi == 0), stop=(mi == 3)), waits=w,
                        inc=(mi == 3))
                v_an = pr.v(lambda v, pm=pm, tl=tl, mh=mh: v.tensor_copy(
                    bv(anat_o + 1024 * tl + 512 * mh, 512), pm[:, 0:512]),
                    waits=[("sT", tm)], inc=True)
        v_st = 0
        for tau in range(4):
            v_st = pr.v(lambda v, tau=tau: v.scalar_tensor_tensor(
                fv(xc_o + 1024 * tau, 1024), bv(anat_o + 1024 * tau, 1024),
                1.0, fv(xc_o + 1024 * tau, 1024), MUL, ADD,
                accum_out=fv(st_o + tau, 1)),
                waits=[("sV", v_an), ("dY", d_xc)], inc=True)
        s_sq = 0
        for tau in range(4):
            s_sq = pr.s(lambda s, tau=tau: s.activation(
                bv(sqs_o, 1024), fv(xc_o + 1024 * tau, 1024), AF.Square,
                accum_out=fv(st_o + 4 + tau, 1)),
                waits=[("sV", v_st)], inc=True)
        pr.v(lambda v: v.tensor_scalar(fv(st_o + 8, 4), fv(st_o, 4),
                                       1.0 / D, None, MUL),
             waits=[("sS", s_sq)], sync=True)
        pr.v(lambda v: v.tensor_tensor(fv(st_o, 4), fv(st_o + 8, 4),
                                       fv(st_o + 8, 4), MUL), sync=True)
        pr.v(lambda v: v.tensor_scalar(fv(st_o + 4, 4), fv(st_o + 4, 4),
                                       1.0 / D, None, MUL), sync=True)
        v_va = pr.v(lambda v: v.scalar_tensor_tensor(
            fv(st_o + 12, 4), fv(st_o + 4, 4), EPS, fv(st_o, 4), ADD, SUB),
            inc=True, sync=True)
        g_rs = pr.g(lambda g: g.tensor_tensor(fv(rs_o, 4), fv(st_o + 12, 4),
                                              fv(nh_o, 4), POW),
                    waits=[("sV", v_va)], inc=True)
        for tau in range(4):
            w = [("sG", g_rs)] if tau == 0 else []
            v_ln1 = pr.v(lambda v, tau=tau: v.tensor_scalar(
                fv(xc_o + 1024 * tau, 1024), fv(xc_o + 1024 * tau, 1024),
                fv(st_o + 8 + tau, 1), fv(rs_o + tau, 1), SUB, MUL),
                waits=w, inc=True, sync=True)
        for tau in range(4):
            pr.ydma(lambda y, T0=T0, tau=tau: y.dma_start(
                x1sp_h[T0 + 128 * tau:T0 + 128 * (tau + 1), :],
                fv(xc_o + 1024 * tau, 1024)), waits=[("sV", v_ln1)])
        if dbg:
            for tau in range(4):
                pr.ydma(lambda y, T0=T0, tau=tau: y.dma_start(
                    dbg_x1[T0 + 128 * tau:T0 + 128 * (tau + 1), :],
                    fv(xc_o + 1024 * tau, 1024)), waits=[("sV", v_ln1)])
            for tau in range(4):
                pr.ydma(lambda y, T0=T0, tau=tau, c=c: y.dma_start(
                    dbg_an[T0 + 128 * tau:T0 + 128 * (tau + 1), 0:1024]
                    .rearrange("p q -> p q"),
                    bv(anat_o + 1024 * tau, 1024)), waits=[("sV", v_ln1)])
            pr.ydma(lambda y, c=c: y.dma_start(
                dbg_yn[:, 4096 * c:4096 * (c + 1)], bv(yTn_o, 4096)),
                waits=[("sV", v_ln1)])
        s_cv = 0
        for tau in range(4):
            s_cv = pr.s(lambda s, tau=tau: s.copy(
                bv(x1nb_o + 1024 * tau, 1024), fv(xc_o + 1024 * tau, 1024)),
                waits=[("sV", v_ln1)], inc=True)
        s_cv_prev = s_cv
        s_x1T = 0
        for k in range(8):
            pm = pm6 if k % 2 == 0 else pm7
            for tau in range(4):
                w = []
                if tau == 0:
                    w = [("sS", s_cv)] + pr.bar("sS", "sV")
                tm = pr.t(lambda t, pm=pm, tau=tau, k=k: t.matmul(
                    pm[:, 128 * tau:128 * (tau + 1)],
                    bv(x1nb_o + 1024 * tau + 128 * k, 128), bv(idn_o, 128),
                    start=(tau == 0), stop=(tau == 3)), waits=w,
                    inc=(tau == 3))
            w = [("sT", tm)]
            if dY_x1T_prev:
                w.append(("dY", dY_x1T_prev))
            s_x1T = pr.s(lambda s, pm=pm, k=k: s.copy(
                bv(x1Ts_o + 512 * k, 512), pm[:, 0:512]), waits=w, inc=True)
            pr.ydma(lambda y, k=k, T0=T0: y.dma_start(
                x1t_h[:, 2048 * k + T0:2048 * k + T0 + 512],
                bv(x1Ts_o + 512 * k, 512)), waits=[("sS", s_x1T)])
        dY_x1T_prev = pr.m("dY")
        if c == 0:
            v_xl = 0
            for k in range(8):
                v_xl = pr.v(lambda v, k=k: v.tensor_copy(
                    bv(x1l_o + 16 * k, 16), bv(x1Ts_o + 512 * k + 1, 16)),
                    waits=[("sS", s_x1T)], inc=True)
            s_xlf = pr.s(lambda s: s.copy(fv(x1f_o, 128), bv(x1l_o, 128)),
                         waits=[("sV", v_xl)], inc=True)
            for j in range(4):
                for k in range(8):
                    w = []
                    if k == 0:
                        w = [("sS", s_xlf)] + pr.bar("sV", "sS")
                    tm = pr.t(lambda t, k=k, j=j: t.matmul(
                        pm6[0:1, 128 * k:128 * (k + 1)] if k < 4 else
                        pm7[0:1, 128 * (k - 4):128 * (k - 3)],
                        bv(x1l_o + 16 * k + 2 * j, 1), bv(idn_o, 128),
                        start=(k % 4 == 0), stop=(k % 4 == 3)), waits=w,
                        inc=(k == 7))
                pr.s(lambda s: s.copy(bv(wst_o, 512, 0, 1), pm6[0:1, 0:512]),
                     waits=[("sT", tm), ("dY", dY_x1T_prev)])
                s_er = pr.s(lambda s: s.copy(bv(wst_o + 512, 512, 0, 1),
                                             pm7[0:1, 0:512]), inc=True)
                pr.t(lambda t: t.matmul(pm6[:, 0:512], bv(tri_o, 128, 0, 1),
                                        bv(wst_o, 512, 0, 1), start=True,
                                        stop=True), waits=[("sS", s_er)])
                t_eb = pr.t(lambda t: t.matmul(pm7[:, 0:512], bv(tri_o, 128, 0, 1),
                                               bv(wst_o + 512, 512, 0, 1),
                                               start=True, stop=True),
                            inc=True)
                pr.v(lambda v: v.tensor_scalar(bv(evb_o, 512), pm6[:, 0:512],
                                               fv(scf_o, 1), None, MUL),
                     waits=[("sT", t_eb), ("sS", s_scl)])
                v_eb = pr.v(lambda v: v.tensor_scalar(
                    bv(evb_o + 512, 512), pm7[:, 0:512], fv(scf_o, 1), None,
                    MUL), inc=True)
                for k in range(8):
                    w = [("sV", v_eb)]
                    if k == 0 and j == 0 and w1_ev:
                        w.append(("sV", w1_ev[-1]))
                    pr.v(lambda v, k=k, j=j: v.tensor_scalar(
                        bv(sqs_o, 1024), bv(evb_o, 1024),
                        fv(x1f_o + 16 * k + 2 * j + 1, 1), None, MUL),
                        waits=w, sync=True)
                    v_w1eff = pr.v(lambda v, k=k, j=j: v.tensor_tensor(
                        bv(w1t_o + 4096 * k + 1024 * j, 1024),
                        bv(w1t_o + 4096 * k + 1024 * j, 1024),
                        bv(sqs_o, 1024), ADD), inc=True, sync=True)

    if upto <= 5:
        finish()
    t_p2 = pr.m("sT")
    v_p2 = pr.m("sV")
    s_p2 = pr.m("sS")
    dY_p2 = pr.m("dY")

    v_w2t = 0
    for r in range(8):
        w = [("sT", t_wst_free[0])]
        if r == 0:
            w += [("sS", s_p2)]
        dm = pr.gdma(lambda g, r=r: g.dma_start(
            bv(wst_o, 4096), w2_h[128 * r:128 * (r + 1), :]), waits=w)
        for cg in range(8):
            pm = pm6 if cg % 2 == 0 else pm7
            for ci in range(4):
                c2 = 4 * cg + ci
                w = []
                if ci == 0:
                    w = pr.bar("sV", "sS")
                    if cg == 0:
                        w.append(("dG", dm))
                tm2 = pr.t(lambda t, pm=pm, ci=ci, c2=c2: t.matmul(
                    pm[:, 128 * ci:128 * (ci + 1)],
                    bv(wst_o + 128 * c2, 128), bv(idn_o, 128),
                    start=(ci == 0), stop=(ci == 3)), waits=w,
                    inc=(ci == 3))
            if cg == 7:
                t_wst_free[0] = tm2
            base = w2t_o + 4096 * cg + 128 * r
            v_w2t = pr.v(lambda v, pm=pm, base=base: v.tensor_copy(
                ab[:, base:base + 4096].rearrange("p (a b) -> p a b", b=1024)
                [:, :, 0:128],
                pm[:, 0:512].rearrange("p (a b) -> p a b", b=128)),
                waits=[("sT", tm2)], inc=True)

    if upto <= 6:
        finish()

    v_w2eff = 0
    for j in range(4):
        for k in range(8):
            w = []
            if k == 0:
                w = [("sT", t_wst_free[0])] + pr.bar("sV", "sS")
            tm = pr.t(lambda t, k=k, j=j: t.matmul(
                pm6[0:1, 128 * k:128 * (k + 1)] if k < 4 else
                pm7[0:1, 128 * (k - 4):128 * (k - 3)],
                bv(x1l_o + 16 * k + 8 + 2 * j, 1), bv(idn_o, 128),
                start=(k % 4 == 0), stop=(k % 4 == 3)), waits=w,
                inc=(k == 7))
        pr.s(lambda s: s.copy(bv(wst_o, 512, 0, 1), pm6[0:1, 0:512]),
             waits=[("sT", tm)])
        s_er = pr.s(lambda s: s.copy(bv(wst_o + 512, 512, 0, 1), pm7[0:1, 0:512]),
                    inc=True)
        pr.t(lambda t: t.matmul(pm6[:, 0:512], bv(tri_o, 128, 0, 1), bv(wst_o, 512, 0, 1),
                                start=True, stop=True), waits=[("sS", s_er)])
        t_eb = pr.t(lambda t: t.matmul(pm7[:, 0:512], bv(tri_o, 128, 0, 1),
                                       bv(wst_o + 512, 512, 0, 1),
                                       start=True, stop=True), inc=True)
        pr.v(lambda v: v.tensor_scalar(bv(evb_o, 512), pm6[:, 0:512],
                                       fv(scf_o, 1), None, MUL),
             waits=[("sT", t_eb), ("sV", v_w2t)])
        v_eb = pr.v(lambda v: v.tensor_scalar(bv(evb_o + 512, 512),
                                              pm7[:, 0:512], fv(scf_o, 1),
                                              None, MUL), inc=True)
        for c2 in range(32):
            r2, kd = c2 // 8, c2 % 8
            oc = x1f_o + 16 * kd + 9 + 2 * j
            pr.v(lambda v, r2=r2, oc=oc: v.tensor_scalar(
                bv(sqs_o, 256).rearrange("p (a b) -> p a b", b=1),
                ab[:, evb_o:evb_o + 1024].rearrange("p (a b) -> p a b", b=4)
                [:, :, r2:r2 + 1],
                fv(oc, 1), None, MUL),
                waits=[("sV", v_eb)] if c2 == 0 else [], sync=True)
            v_w2eff = pr.v(lambda v, c2=c2, j=j: v.tensor_tensor(
                bv(w2t_o + 1024 * c2 + 256 * j, 256),
                bv(w2t_o + 1024 * c2 + 256 * j, 256), bv(sqs_o, 256), ADD),
                inc=True, sync=True)

    if upto <= 7:
        finish()

    v_ln2 = 0
    d_y_final = 0
    gelu_last = 0
    for fc in range(4):
        if upto <= 7 + fc:
            break
        F0 = ACH * fc
        d_x1c = 0
        for k in range(8):
            w = []
            if fc == 0 and k == 0:
                w = [("sT", t_p2), ("sV", v_p2), ("sS", s_p2)]
            elif k == 0:
                w = [("sS", gelu_last)]
            d_x1c = pr.ydma(lambda y, k=k, F0=F0: y.dma_start(
                bv(x1c_o + 512 * k, 512),
                x1t_h[:, 2048 * k + F0:2048 * k + F0 + 512]), waits=w)
        d_x1r = 0
        for tau in range(4):
            w = []
            if fc == 0 and tau == 0:
                w = [("sS", s_p2)]
            elif tau == 0:
                w = [("sV", v_ln2)]
            d_x1r = pr.ydma(lambda y, tau=tau, F0=F0: y.dma_start(
                fv(xc_o + 1024 * tau, 1024),
                x1sp_h[F0 + 128 * tau:F0 + 128 * (tau + 1), :]), waits=w)
        for mo in range(32):
            i, pm, c0 = next_half()
            for k in range(8):
                w = []
                if k == 0:
                    w = [("dY", d_x1c)]
                    if fc == 0 and mo == 0:
                        w.append(("sV", v_w1eff))
                    if hstate["ev"][i]:
                        w.append(("sS", hstate["ev"][i]))
                tm = pr.t(lambda t, pm=pm, c0=c0, k=k, mo=mo: t.matmul(
                    pm[:, c0:c0 + 512],
                    bv(w1t_o + 4096 * k + 128 * mo, 128),
                    bv(x1c_o + 512 * k, 512),
                    start=(k == 0), stop=(k == 7)), waits=w,
                    inc=(k == 7))
            w = [("sT", tm)]
            if fc == 0 and mo == 0:
                w += [("sS", s_p2), ("sV", v_p2)]
            gelu_last = pr.s(lambda s, pm=pm, c0=c0, mo=mo: s.activation(
                bv(hT_o + 512 * mo, 512), pm[:, c0:c0 + 512], AF.Gelu,
                bias=fv(b1s_o + mo, 1)), waits=w, inc=True)
            hstate["ev"][i] = gelu_last
        if dbg and fc == 0:
            pr.ydma(lambda y: y.dma_start(dbg_h[:, :], bv(hT_o, 16384)),
                    waits=[("sS", gelu_last)])
            pr.ydma(lambda y: y.dma_start(dbg_x1c[:, :], bv(x1c_o, 4096)),
                    waits=[("dY", d_x1c)])
            pr.ydma(lambda y: y.dma_start(dbg_w1t[:, :], bv(w1t_o, 32768)))
            pr.ydma(lambda y: y.dma_start(dbg_w2t[:, :], bv(w2t_o, 32768)))
        s_f2 = 0
        for mo in range(8):
            i, pm, c0 = next_half()
            for c2 in range(32):
                w = []
                if c2 == 0:
                    w = [("sS", gelu_last)]
                    if fc == 0 and mo == 0:
                        w.append(("sV", v_w2eff))
                    if hstate["ev"][i]:
                        w.append(("sS", hstate["ev"][i]))
                tm = pr.t(lambda t, pm=pm, c0=c0, c2=c2, mo=mo: t.matmul(
                    pm[:, c0:c0 + 512],
                    bv(w2t_o + 1024 * c2 + 128 * mo, 128),
                    bv(hT_o + 512 * c2, 512),
                    start=(c2 == 0), stop=(c2 == 31)), waits=w,
                    inc=(c2 == 31))
            w = [("sT", tm)]
            if fc == 0 and mo == 0:
                w.append(("dY", dY_p2))
            s_f2 = pr.s(lambda s, pm=pm, c0=c0, mo=mo: s.activation(
                bv(oTe_o + 512 * mo, 512), pm[:, c0:c0 + 512], AF.Identity,
                bias=fv(b2s_o + mo, 1)), waits=w, inc=True)
            hstate["ev"][i] = s_f2
        if dbg and fc == 0:
            pr.ydma(lambda y: y.dma_start(dbg_o[:, :], bv(oTe_o, 4096)),
                    waits=[("sS", s_f2)])
        v_st = 0
        for tl in range(4):
            for mh in range(2):
                pm = pm6 if (2 * tl + mh) % 2 == 0 else pm7
                for mi in range(4):
                    mo = 4 * mh + mi
                    w = []
                    if mi == 0:
                        w = [("sS", s_f2)] + pr.bar("sV")
                        if fc == 0 and tl == 0 and mh == 0:
                            w.append(("sT", t_wst_free[0]))
                    tm = pr.t(lambda t, pm=pm, mi=mi, mo=mo, tl=tl: t.matmul(
                        pm[:, 128 * mi:128 * (mi + 1)],
                        bv(oTe_o + 512 * mo + 128 * tl, 128), bv(idn_o, 128),
                        start=(mi == 0), stop=(mi == 3)), waits=w,
                        inc=(mi == 3))
                v_st = pr.v(lambda v, pm=pm, tl=tl, mh=mh:
                            v.scalar_tensor_tensor(
                                fv(xc_o + 1024 * tl + 512 * mh, 512),
                                pm[:, 0:512], 1.0,
                                fv(xc_o + 1024 * tl + 512 * mh, 512),
                                MUL, ADD,
                                accum_out=fv(st_o + 16 + 2 * tl + mh, 1)),
                            waits=[("sT", tm), ("dY", d_x1r)], inc=True)
        v_st = pr.v(lambda v: v.tensor_tensor(
            fv(st_o, 4),
            af[:, st_o + 16:st_o + 24].rearrange("p (a b) -> p a b", b=2)
            [:, :, 0:1],
            af[:, st_o + 16:st_o + 24].rearrange("p (a b) -> p a b", b=2)
            [:, :, 1:2], ADD), waits=[("sV", v_st)], inc=True, sync=True)
        s_sq4 = 0
        for tau in range(4):
            w = [("sV", v_st)]
            s_sq4 = pr.s(lambda s, tau=tau: s.activation(
                bv(evb_o, 1024), fv(xc_o + 1024 * tau, 1024), AF.Square,
                accum_out=fv(st_o + 4 + tau, 1)), waits=w, inc=True)
        pr.v(lambda v: v.tensor_scalar(fv(st_o + 8, 4), fv(st_o, 4),
                                       1.0 / D, None, MUL),
             waits=[("sS", s_sq4)], sync=True)
        pr.v(lambda v: v.tensor_tensor(fv(st_o, 4), fv(st_o + 8, 4),
                                       fv(st_o + 8, 4), MUL), sync=True)
        pr.v(lambda v: v.tensor_scalar(fv(st_o + 4, 4), fv(st_o + 4, 4),
                                       1.0 / D, None, MUL), sync=True)
        v_va = pr.v(lambda v: v.scalar_tensor_tensor(
            fv(st_o + 12, 4), fv(st_o + 4, 4), EPS, fv(st_o, 4), ADD, SUB),
            inc=True, sync=True)
        g_rs = pr.g(lambda g: g.tensor_tensor(fv(rs_o, 4), fv(st_o + 12, 4),
                                              fv(nh_o, 4), POW),
                    waits=[("sV", v_va)], inc=True)
        for tau in range(4):
            w = [("sG", g_rs)] if tau == 0 else []
            v_ln2 = pr.v(lambda v, tau=tau: v.tensor_scalar(
                fv(xc_o + 1024 * tau, 1024), fv(xc_o + 1024 * tau, 1024),
                fv(st_o + 8 + tau, 1), fv(rs_o + tau, 1), SUB, MUL),
                waits=w, inc=True, sync=True)
        for tau in range(4):
            d_y_final = pr.ydma(lambda y, F0=F0, tau=tau: y.dma_start(
                y_h[F0 + 128 * tau:F0 + 128 * (tau + 1), :],
                fv(xc_o + 1024 * tau, 1024)), waits=[("sV", v_ln2)])

    if upto > 10:
        pr.op("Y", lambda y: y.nop(), [("dY", d_y_final)], ())
    else:
        finish()

    with (
        nc.semaphore("hsT") as hsT,
        nc.semaphore("hsV") as hsV,
        nc.semaphore("hsS") as hsS,
        nc.semaphore("hsG") as hsG,
        nc.semaphore("hdG") as hdG,
        nc.semaphore("hdY") as hdY,
        nc.Block() as blk,
    ):
        smap = {"sT": hsT, "sV": hsV, "sS": hsS, "sG": hsG,
                "dG": hdG, "dY": hdY}

        def replay(eng_key):
            def inner(eng):
                for fn, waits, incs in pr.ops[eng_key]:
                    for s, vv in waits:
                        eng.wait_ge(smap[s], vv)
                    ins = fn(eng)
                    for s, n in incs:
                        ins.then_inc(smap[s], n)
            return inner

        blk.tensor(replay("T"))
        blk.vector(replay("V"))
        blk.scalar(replay("S"))
        blk.gpsimd(replay("G"))
        blk.sync(replay("Y"))
    return nc


_NC_CACHE = {}


def kernel(**inputs):
    x = np.ascontiguousarray(np.asarray(inputs["L0_attn_x"]))
    wc = np.ascontiguousarray(np.asarray(inputs["L0_attn_Wc_down"]))
    wq = np.ascontiguousarray(np.asarray(inputs["L0_attn_Wq_up"]))
    wkv = np.ascontiguousarray(np.asarray(inputs["L0_attn_Wkv_up"]))
    wp = np.ascontiguousarray(np.asarray(inputs["L0_attn_Wc_proj"]))
    w1 = np.ascontiguousarray(np.asarray(inputs["L0_ffn_W1"]))
    b1 = np.ascontiguousarray(np.asarray(inputs["L0_ffn_b1"]))
    w2 = np.ascontiguousarray(np.asarray(inputs["L0_ffn_W2"]))
    b2 = np.ascontiguousarray(np.asarray(inputs["L0_ffn_b2"]))
    scl = np.ascontiguousarray(np.asarray(inputs["L0_ffn_scale"],
                                          dtype=np.float32).reshape(1, 1))
    if "nc" not in _NC_CACHE:
        _NC_CACHE["nc"] = build_program(paranoid="G:dG,dY")
    nc = _NC_CACHE["nc"]
    in_maps = [
        {"x": x[i], "wc": wc, "wq": wq, "wkv": wkv, "wp": wp,
         "w1": w1, "b1": b1, "w2": w2, "b2": b2, "scl": scl}
        for i in range(8)
    ]
    res = run_bass_kernel_spmd(nc, in_maps, core_ids=list(range(8)))
    return np.stack([np.asarray(res.results[i]["y"]) for i in range(8)],
                    axis=0)
